# version 03
# baseline (speedup 1.0000x reference)
import contextlib
import numpy as np
import ml_dtypes
import concourse.bass as bass
import concourse.mybir as mybir
from concourse.bass_utils import run_bass_kernel_spmd

F32 = mybir.dt.float32
BF16 = mybir.dt.bfloat16
I32 = mybir.dt.int32
AF = mybir.ActivationFunctionType
ALU = mybir.AluOpType
BF16NP = ml_dtypes.bfloat16

D = 1024
T = 8192
B = 4
TH = T // 2


class Sem:
    def __init__(self, h):
        self.h = h
        self.n = 0

    def bump(self, k=1):
        self.n += k
        return k


class Ctx:
    def __init__(self, nc, block, stack):
        self.nc = nc
        self.block = block
        self.stack = stack
        self.sem_stack = stack
        self.sems = {}

    def sem(self, name):
        if name not in self.sems:
            h = self.sem_stack.enter_context(self.nc.semaphore(name))
            self.sems[name] = Sem(h)
        return self.sems[name]

    def barrier(self):
        b = self.sem("BAR")
        target = b.n + 5
        b.n = target
        for en in ("tensor", "vector", "scalar", "gpsimd", "sync"):
            def _f(e, target=target):
                e.drain()
                e.nop().then_inc(b.h, 1)
                e.wait_ge(b.h, target)
            getattr(self.block, en)(_f)



def emit_l0_ffn(ctx, io):
    nc, block = ctx.nc, ctx.block
    F = 4096
    NT = TH // 1024
    sL = ctx.sem("ffL"); sT = ctx.sem("ffT"); sV = ctx.sem("ffV")
    sS = ctx.sem("ffS"); sO = ctx.sem("ffO"); sW = ctx.sem("ffW")

    sb = lambda n, sh, dt: ctx.stack.enter_context(nc.sbuf_tensor(n, sh, dt))
    ps_ = lambda n, sh: ctx.stack.enter_context(nc.psum_tensor(n, sh, F32))
    w1s = sb("ff_w1", [128, 8 * F], BF16)

    w2a = sb("ff_w2a", [128, 1024], BF16)
    w2b = sb("ff_w2b", [128, 1024], BF16)
    xbf = sb("ff_xbf", [128, 8 * 1024], BF16)
    xf = sb("ff_xf", [128, 8 * 1024], F32)
    abf = sb("ff_abf", [128, 8 * 1024], BF16)
    y1s = sb("ff_y1", [128, 32 * 1024], BF16)
    qi = y1s[:, 0:8192].bitcast(I32)
    scb = y1s[:, 8192:16384].bitcast(F32)
    sq = y1s[:, 16384:24576]
    rows = y1s[:, 24576:32768].bitcast(F32)[0:1, :]
    rbf = sb("ff_rbf", [1, 3072], BF16)
    gb = sb("ff_gb", [128, 16], F32)
    b1c = sb("ff_b1c", [128, 32], F32)
    b2h = sb("ff_b2h", [128, 8], F32)
    tmp512 = sb("ff_tmp", [128, 512], F32)
    onesc = sb("ff_ones", [128, 1], BF16)
    onesr = sb("ff_onesr", [1, 128], BF16)
    cst = sb("ff_cst", [128, 8], F32)
    pA = ps_("ff_pA", [128, 2048])
    pB = ps_("ff_pB", [128, 2048])
    if True:
        @block.sync
        def _(sync):
            sync.dma_start(out=gb[:, 0:8], in_=io["gamma_col"]).then_inc(sL.h, sL.bump(16))
            sync.dma_start(out=gb[:, 8:16], in_=io["beta_col"]).then_inc(sL.h, sL.bump(16))
            sync.dma_start(out=b1c[:], in_=io["fc1_bias"]).then_inc(sL.h, sL.bump(16))
            sync.dma_start(out=b2h[:], in_=io["fc2_bias"]).then_inc(sL.h, sL.bump(16))
            sync.dma_start(out=onesc[:], in_=io["ones_bf"]).then_inc(sL.h, sL.bump(16))
            sync.dma_start(out=onesr[:], in_=io["ones_row_bf"]).then_inc(sL.h, sL.bump(16))
            sync.dma_start(out=cst[:], in_=io["consts"]).then_inc(sL.h, sL.bump(16))
            sync.wait_ge(sL.h, sL.n)

        ctx.barrier()

        @block.vector
        def _(v):
            v.tensor_scalar(b2h[:], b2h[:], 0.5, None, ALU.mult)

        for g2 in range(32):
            @block.sync
            def _(sync, g2=g2):
                sync.wait_ge(sW.h, sW.n)
                sync.dma_start(out=qi[:, 0:1024], in_=io["fc2_q"][g2 * 128:(g2 + 1) * 128, :]).then_inc(sL.h, sL.bump(16))
                sync.dma_start(out=scb[:, 0:1024],
                               in_=bass.AP(io["fc2_scales"].tensor, g2 * 1024, [[0, 128], [1, 1024]])).then_inc(sL.h, sL.bump(16))

            @block.vector
            def _(v, g2=g2):
                v.wait_ge(sL.h, sL.n)
                v.scalar_tensor_tensor(w2a[:], qi[:, 0:1024], -128.0, scb[:, 0:1024],
                                       ALU.add, ALU.mult).then_inc(sV.h, sV.bump())

            @block.scalar
            def _(s, g2=g2):
                s.wait_ge(sV.h, sV.n)
                s.dma_start(out=io["w2_scratch"][g2 * 128:(g2 + 1) * 128, :], in_=w2a[:]).then_inc(sW.h, sW.bump(16))

        @block.scalar
        def _(s):
            s.wait_ge(sW.h, sW.n)

        ctx.barrier()

        for g in range(8):
            @block.sync
            def _(sync, g=g):
                sync.wait_ge(sV.h, sV.n)
                sync.dma_start(out=qi, in_=io["fc1_q"][g * 128:(g + 1) * 128, :]).then_inc(sL.h, sL.bump(16))
                sync.dma_start(out=scb,
                               in_=bass.AP(io["fc1_scales"].tensor, g * F, [[0, 128], [1, F]])).then_inc(sL.h, sL.bump(16))

            @block.vector
            def _(v, g=g):
                v.wait_ge(sL.h, sL.n)
                v.scalar_tensor_tensor(w1s[:, g * F:(g + 1) * F], qi, -128.0, scb,
                                       ALU.add, ALU.mult).then_inc(sV.h, sV.bump())

        ctx.barrier()

        for t in range(NT):
            @block.sync
            def _(sync, t=t):
                src = bass.AP(io["xin"].tensor, t * 1024, [[TH, 128], [128 * TH, 8], [1, 1024]])
                dst = bass.AP(xf, 0, [[8 * 1024, 128], [1024, 8], [1, 1024]])
                sync.dma_start(out=dst, in_=src).then_inc(sL.h, sL.bump(16))
                sync.wait_ge(sL.h, sL.n)

            @block.gpsimd
            def _(g, t=t):
                src = bass.AP(io["xin"].tensor, t * 1024, [[TH, 128], [128 * TH, 8], [1, 1024]])
                dst = bass.AP(xbf, 0, [[8 * 1024, 128], [1024, 8], [1, 1024]])
                g.dma_start(out=dst, in_=src).then_inc(sL.h, sL.bump(16))
                g.wait_ge(sL.h, sL.n)

            ctx.barrier()

            @block.scalar
            def _(s, t=t):
                s.activation(sq, xbf[:], AF.Square).then_inc(sS.h, sS.bump())

            @block.tensor
            def _(tt, t=t):
                for half in range(2):
                    c0 = half * 512
                    for dt in range(8):
                        tt.matmul(pA[0:1, c0:c0 + 512], onesc[:], xbf[:, dt * 1024 + c0: dt * 1024 + c0 + 512],
                                  start=(dt == 0), stop=(dt == 7))
                tt.wait_ge(sS.h, sS.n)
                for half in range(2):
                    c0 = half * 512
                    for dt in range(8):
                        m = tt.matmul(pA[0:1, 1024 + c0:1024 + c0 + 512], onesc[:],
                                      sq[:, dt * 1024 + c0: dt * 1024 + c0 + 512],
                                      start=(dt == 0), stop=(dt == 7))
                        if half == 1 and dt == 7:
                            m.then_inc(sT.h, sT.bump())

            MU, VAR, RSTD, TMP = 0, 1024, 2048, 3072

            @block.vector
            def _(v, t=t):
                v.wait_ge(sT.h, sT.n)
                v.tensor_scalar(rows[:, MU:MU + 1024], pA[0:1, 0:1024], 1.0 / D, None, ALU.mult)
                v.tensor_scalar(rows[:, TMP:TMP + 1024], pA[0:1, 1024:2048], 1.0 / D, None, ALU.mult)
                v.tensor_tensor(rows[:, VAR:VAR + 1024], rows[:, MU:MU + 1024], rows[:, MU:MU + 1024], ALU.mult)
                v.tensor_tensor(rows[:, VAR:VAR + 1024], rows[:, TMP:TMP + 1024], rows[:, VAR:VAR + 1024],
                                ALU.subtract).then_inc(sV.h, sV.bump())

            @block.scalar
            def _(s, t=t):
                s.wait_ge(sV.h, sV.n)
                s.activation(rows[:, TMP:TMP + 1024], rows[:, VAR:VAR + 1024], AF.Ln, bias=cst[0:1, 0:1]).then_inc(sS.h, sS.bump())
                s.wait_ge(sS.h, sS.n)
                s.activation(rows[:, RSTD:RSTD + 1024], rows[:, TMP:TMP + 1024], AF.Exp,
                             scale=-0.5).then_inc(sS.h, sS.bump())

            @block.vector
            def _(v, t=t):
                v.wait_ge(sS.h, sS.n)
                v.tensor_tensor(rbf[:, 1024:2048], rows[:, MU:MU + 1024],
                                rows[:, RSTD:RSTD + 1024], ALU.mult)
                v.tensor_scalar(rows[:, TMP:TMP + 1024], rows[:, VAR:VAR + 1024], cst[0:1, 0:1], None, ALU.add)
                v.reciprocal(rows[:, TMP:TMP + 1024], rows[:, TMP:TMP + 1024])
                v.tensor_tensor(rows[:, TMP:TMP + 1024], rows[:, VAR:VAR + 1024],
                                rows[:, TMP:TMP + 1024], ALU.mult).then_inc(sV.h, sV.bump())

            @block.scalar
            def _(s, t=t):
                s.wait_ge(sV.h, sV.n)
                s.activation(rows[:, TMP:TMP + 1024], rows[:, TMP:TMP + 1024], AF.Ln, bias=cst[0:1, 0:1]).then_inc(sS.h, sS.bump())
                s.wait_ge(sS.h, sS.n)
                s.activation(rows[:, TMP:TMP + 1024], rows[:, TMP:TMP + 1024], AF.Exp,
                             scale=-0.5).then_inc(sS.h, sS.bump())

            @block.vector
            def _(v, t=t):
                v.wait_ge(sS.h, sS.n)
                v.tensor_copy(rbf[:, 0:1024], rows[:, RSTD:RSTD + 1024])
                v.tensor_copy(rbf[:, 2048:3072], rows[:, TMP:TMP + 1024]).then_inc(sV.h, sV.bump())

            @block.tensor
            def _(tt, t=t):
                tt.wait_ge(sV.h, sV.n)
                for q in range(6):
                    ps = pB[:, q * 512:(q + 1) * 512] if q < 4 else pA[:, (q - 4) * 512:(q - 3) * 512]
                    m = tt.matmul(ps, onesr[:], rbf[:, q * 512:(q + 1) * 512],
                                  start=True, stop=True)
                    if q == 5:
                        m.then_inc(sT.h, sT.bump())

            @block.vector
            def _(v, t=t):
                v.wait_ge(sT.h, sT.n)
                for dt in range(8):
                    sl = slice(dt * 1024, (dt + 1) * 1024)
                    v.tensor_tensor(abf[:, sl], xbf[:, sl], pB[:, 0:1024], ALU.mult)
                    v.tensor_tensor(abf[:, sl], abf[:, sl], pB[:, 1024:2048], ALU.subtract)
                    v.tensor_tensor(xbf[:, sl], abf[:, sl], pA[:, 0:1024], ALU.mult)
                    v.tensor_scalar(xbf[:, sl], xbf[:, sl], gb[:, dt:dt + 1], gb[:, 8 + dt:8 + dt + 1],
                                    ALU.mult, ALU.add)

            ctx.barrier()

            NSIL = 64
            for f in range(32):
                for hh in range(2):
                    k = f * 2 + hh
                    @block.tensor
                    def _(tt, f=f, hh=hh, k=k, t=t):
                        if k >= 4:
                            tt.wait_ge(sS.h, sS.n - 3)
                        ps = pA[:, (k % 4) * 512:(k % 4 + 1) * 512]
                        for g in range(8):
                            m = tt.matmul(ps, w1s[:, g * F + f * 128: g * F + (f + 1) * 128],
                                          xbf[:, g * 1024 + hh * 512: g * 1024 + hh * 512 + 512],
                                          start=(g == 0), stop=(g == 7))
                            if g == 7:
                                m.then_inc(sT.h, sT.bump())

                    @block.scalar
                    def _(s, f=f, hh=hh, k=k, t=t):
                        s.wait_ge(sT.h, sT.n)
                        ps = pA[:, (k % 4) * 512:(k % 4 + 1) * 512]
                        s.activation(y1s[:, f * 1024 + hh * 512: f * 1024 + hh * 512 + 512], ps, AF.Silu,
                                     bias=b1c[:, f:f + 1]).then_inc(sS.h, sS.bump())

            @block.scalar
            def _(s, t=t):
                s.wait_ge(sS.h, sS.n)

            ctx.barrier()

            for sub in range(2):
                for kf in range(32):
                    @block.gpsimd
                    def _(g, sub=sub, kf=kf, t=t):
                        buf = w2a if kf % 2 == 0 else w2b
                        if sub > 0 or kf >= 2:
                            g.wait_ge(sT.h, sT.n - 1)
                        g.dma_start(out=buf[:], in_=io["w2_scratch"][kf * 128:(kf + 1) * 128, :]).then_inc(
                            sW.h, sW.bump(16))

                    @block.tensor
                    def _(tt, sub=sub, kf=kf, t=t):
                        buf = w2a if kf % 2 == 0 else w2b
                        tt.wait_ge(sW.h, sW.n)
                        for do in range(8):
                            ps = (pA if do < 4 else pB)[:, (do % 4) * 512:(do % 4 + 1) * 512]
                            m = tt.matmul(ps, buf[:, do * 128:(do + 1) * 128],
                                          y1s[:, kf * 1024 + sub * 512: kf * 1024 + sub * 512 + 512],
                                          start=(kf == 0), stop=(kf == 31))
                            if do == 7:
                                m.then_inc(sT.h, sT.bump())

                @block.vector
                def _(v, sub=sub, t=t):
                    v.wait_ge(sT.h, sT.n)
                    for do in range(8):
                        ps = (pA if do < 4 else pB)[:, (do % 4) * 512:(do % 4 + 1) * 512]
                        sl = slice(do * 1024 + sub * 512, do * 1024 + sub * 512 + 512)
                        v.tensor_scalar(tmp512[:], ps, 0.5, b2h[:, do:do + 1], ALU.mult, ALU.add)
                        v.tensor_tensor(tmp512[:], tmp512[:], abf[:, sl], ALU.add)
                        v.tensor_tensor(xf[:, sl], xf[:, sl], tmp512[:], ALU.add)

                ctx.barrier()

            @block.sync
            def _(sync, t=t):
                dst = bass.AP(io["xout"].tensor, t * 1024, [[TH, 128], [128 * TH, 8], [1, 1024]])
                src = bass.AP(xf, 0, [[8 * 1024, 128], [1024, 8], [1, 1024]])
                sync.dma_start(out=dst, in_=src).then_inc(sO.h, sO.bump(16))
                sync.wait_ge(sO.h, sO.n)

            ctx.barrier()


L0FFN_INPUT_SPECS = [
    ("xin", [D, TH], F32),
    ("fc1_q", [1024, 4096], I32),
    ("fc1_scales", [8, 4096], F32),
    ("fc1_bias", [128, 32], F32),
    ("fc2_q", [4096, 1024], I32),
    ("fc2_scales", [32, 1024], F32),
    ("fc2_bias", [128, 8], F32),
    ("gamma_col", [128, 8], F32),
    ("beta_col", [128, 8], F32),
    ("ones_bf", [128, 1], BF16),
    ("ones_row_bf", [1, 128], BF16),
    ("consts", [128, 8], F32),
]


def host_l0_ffn_inputs(core, x1_full, inp):
    b, h = core // 2, core % 2
    return dict(
        xin=np.ascontiguousarray(x1_full[b, h * TH:(h + 1) * TH, :].T),
        fc1_q=np.ascontiguousarray(np.asarray(inp["L0_ffn_fc1_q"], np.int32)),
        fc1_scales=np.ascontiguousarray(np.asarray(inp["L0_ffn_fc1_scales"], np.float32)),
        fc1_bias=np.ascontiguousarray(np.asarray(inp["L0_ffn_fc1_bias"], np.float32).reshape(32, 128).T),
        fc2_q=np.ascontiguousarray(np.asarray(inp["L0_ffn_fc2_q"], np.int32)),
        fc2_scales=np.ascontiguousarray(np.asarray(inp["L0_ffn_fc2_scales"], np.float32)),
        fc2_bias=np.ascontiguousarray(np.asarray(inp["L0_ffn_fc2_bias"], np.float32).reshape(8, 128).T),
        gamma_col=np.ascontiguousarray(np.asarray(inp["L0_ffn_ln_gamma"], np.float32).reshape(8, 128).T),
        beta_col=np.ascontiguousarray(np.asarray(inp["L0_ffn_ln_beta"], np.float32).reshape(8, 128).T),
        ones_bf=np.ones((128, 1), BF16NP),
        ones_row_bf=np.ones((1, 128), BF16NP),
        consts=np.tile(np.array([[1e-5, 0, 0, 0, 0, 0, 0, 0]], np.float32), (128, 1)),
    )


def build_l0_ffn_launch():
    nc = bass.Bass("TRN2", target_bir_lowering=False, debug=False)
    io = {}
    for name, shape, dt in L0FFN_INPUT_SPECS:
        io[name] = nc.dram_tensor(name, shape, dt, kind="ExternalInput").ap()
    io["xout"] = nc.dram_tensor("xout", [D, TH], F32, kind="ExternalOutput").ap()
    io["w2_scratch"] = nc.dram_tensor("w2_scratch", [4096, 1024], BF16, kind="Internal").ap()
    with contextlib.ExitStack() as stack:
        block = stack.enter_context(nc.Block())
        ctx = Ctx(nc, block, stack)
        emit_l0_ffn(ctx, io)
    return nc



def emit_l0_attn(ctx, io):
    nc, block = ctx.nc, ctx.block
    NB = T // 512
    sL = ctx.sem("aL"); sT = ctx.sem("aT"); sV = ctx.sem("aV")
    sS = ctx.sem("aS"); sO = ctx.sem("aO"); sG = ctx.sem("aG")
    sb = lambda n, sh, dt: ctx.stack.enter_context(nc.sbuf_tensor(n, sh, dt))
    ps_ = lambda n, sh, dt=F32: ctx.stack.enter_context(nc.psum_tensor(n, sh, dt))

    W = {nm: sb("a_" + nm, [128, 8 * 512], BF16) for nm in ("wq", "wk", "wv", "wb", "wa")}
    xbf = sb("a_xbf", [128, 8 * 512], BF16)
    xown = sb("a_xown", [128, 4 * 512], F32)
    lnx = sb("a_lnx", [128, 8 * 512], BF16)
    sqb = lnx
    qf = sb("a_qf", [128, 4 * 512], BF16)
    kf = sb("a_kf", [128, 4 * 512], BF16)
    vtm = sb("a_vtm", [128, 4 * 512], F32)
    btm = sb("a_btm", [128, 4 * 512], F32)
    atm = sb("a_atm", [128, 4 * 512], F32)
    asc = sb("a_asc", [128, 4 * 512], F32)
    vb = sb("a_vb", [128, 4 * 512], BF16)
    vbm = sb("a_vbm", [128, 512], BF16)
    lsum = sb("a_lsum", [128, 32], F32)
    lam = sb("a_lam", [128, 32], BF16)
    bms = sb("a_bms", [128, 32], F32)
    mcol = sb("a_mcol", [128, 16], F32)
    curvec = sb("a_curvec", [128, 8], F32)
    mbt16 = sb("a_mbt16", [16, 128], F32)
    stage = sb("a_stage", [1, 2048], F32)
    stg2b = sb("a_stg2", [1, 4096], BF16)
    cfb = sb("a_cfb", [1, 1024], BF16)
    bmb = sb("a_bmb", [1, 1024], BF16)
    c8 = sb("a_c8", [8, 128], BF16)
    b8 = sb("a_b8", [8, 128], BF16)
    lhsA = sb("a_lhsA", [4, 1024], BF16)
    rhsA = sb("a_rhsA", [4, 1024], BF16)
    dec = sb("a_dec", [128, 512], F32)
    st4 = sb("a_st4", [128, 512], BF16)
    stpf = sb("a_stpf", [128, 256], F32)
    stpb = sb("a_stpb", [128, 256], BF16)
    ktm = sb("a_ktm", [128, 512], BF16)
    ysb = sb("a_ysb", [128, 512], F32)
    memc = sb("a_memc", [128, 256], F32)
    x1c = sb("a_x1c", [128, 512], F32)
    ctm = sb("a_ctm", [128, 512], BF16)
    asumc = sb("a_asum", [128, 4], F32)
    bvab = sb("a_bvab", [128, 3 * 512], F32)
    bqkc = sb("a_bqk", [128, 8], F32)
    rows = sb("a_rows", [1, 1024], F32)
    rbf = sb("a_rbf", [1, 1024], BF16)
    onesc = sb("a_onesc", [128, 1], BF16)
    onesr = sb("a_onesr", [1, 128], BF16)
    cst = sb("a_cst", [128, 8], F32)
    trilc = sb("a_tril", [128, 128], BF16)
    mneg = sb("a_mneg", [128, 512], F32)
    sel8 = sb("a_sel8", [8, 512], BF16)
    idf = sb("a_idf", [128, 128], F32)
    idb = sb("a_idb", [128, 128], BF16)
    indr = sb("a_indr", [1, 256], BF16)
    bvarow = sb("a_bvarow", [1, 1536], F32)
    onesrf = sb("a_onesrf", [1, 128], F32)

    pq1 = ps_("a_pq1", [128, 512])
    pq2 = ps_("a_pq2", [128, 512])
    pq3 = ps_("a_pq3", [128, 512])
    pq4 = ps_("a_pq4", [128, 512])
    pq5 = ps_("a_pq5", [128, 512])
    pq6 = ps_("a_pq6", [128, 512])
    pq7 = ps_("a_pq7", [128, 512])
    pqb = ps_("a_pqb", [128, 1024], BF16)

    MU, TMP = 0, 512

    @block.sync
    def _(sync):
        for key, dst in (("ctril", trilc), ("cmask", mneg), ("csel", sel8),
                         ("cidf", idf), ("cidb", idb), ("cind", indr),
                         ("consts", cst), ("bqk_col", bqkc), ("bva_row", bvarow),
                         ("ones_bf", onesc), ("ones_row_bf", onesr), ("ones_row_f", onesrf)):
            sync.dma_start(out=dst[:], in_=io[key]).then_inc(sL.h, sL.bump(16))
        sync.wait_ge(sL.h, sL.n)

    @block.gpsimd
    def _(g):
        for nm in ("wq", "wk", "wv", "wb", "wa"):
            src = bass.AP(io[nm].tensor, 0, [[512, 128], [128 * 512, 8], [1, 512]])
            dst = bass.AP(W[nm], 0, [[8 * 512, 128], [512, 8], [1, 512]])
            g.dma_start(out=dst, in_=src).then_inc(sG.h, sG.bump(16))
        g.wait_ge(sG.h, sG.n)

    ctx.barrier()

    @block.gpsimd
    def _(g):
        src = bass.AP(io["cones2"].tensor, 0, [[1024, 2], [1, 1024]])
        g.dma_start(out=bass.AP(lhsA, 2 * 1024, [[1024, 2], [1, 1024]]), in_=src).then_inc(sG.h, sG.bump(16))
        g.dma_start(out=bass.AP(rhsA, 0, [[1024, 2], [1, 1024]]), in_=src).then_inc(sG.h, sG.bump(16))
        g.wait_ge(sG.h, sG.n)

    @block.tensor
    def _(tt):
        for i, pq in enumerate((pq1, pq2, pq3)):
            m = tt.matmul(pq[:, 0:512], onesrf[:], bvarow[0:1, i * 512:(i + 1) * 512],
                          start=True, stop=True)
        m.then_inc(sT.h, sT.bump())

    @block.vector
    def _(v):
        v.wait_ge(sT.h, sT.n)
        v.tensor_copy(bvab[:, 0:512], pq1[:, 0:512])
        v.tensor_copy(bvab[:, 512:1024], pq2[:, 0:512])
        v.tensor_copy(bvab[:, 1024:1536], pq3[:, 0:512])
        v.memset(stpf[:], 0.0)
        v.memset(stpb[:], 0.0)
        v.tensor_scalar(bqkc[:], bqkc[:], 0.125, None, ALU.mult).then_inc(sV.h, sV.bump())

    ctx.barrier()

    for blk in range(NB):
        tok0 = blk * 512

        @block.sync
        def _(sync, tok0=tok0):
            src = bass.AP(io["xownT"].tensor, tok0, [[T, 128], [128 * T, 4], [1, 512]])
            dst = bass.AP(xown, 0, [[4 * 512, 128], [512, 4], [1, 512]])
            sync.dma_start(out=dst, in_=src).then_inc(sL.h, sL.bump(16))
            sync.wait_ge(sL.h, sL.n)

        @block.gpsimd
        def _(g, tok0=tok0):
            src = bass.AP(io["xT"].tensor, tok0, [[T, 128], [128 * T, 8], [1, 512]])
            dst = bass.AP(xbf, 0, [[8 * 512, 128], [512, 8], [1, 512]])
            g.dma_start(out=dst, in_=src).then_inc(sG.h, sG.bump(16))
            g.wait_ge(sG.h, sG.n)

        ctx.barrier()

        @block.scalar
        def _(s):
            s.activation(sqb[:], xbf[:], AF.Square).then_inc(sS.h, sS.bump())

        @block.tensor
        def _(tt):
            for dt in range(8):
                tt.matmul(pq7[0:1, 0:512], onesc[:], xbf[:, dt * 512:(dt + 1) * 512],
                          start=(dt == 0), stop=(dt == 7))
            tt.wait_ge(sS.h, sS.n)
            for dt in range(8):
                m = tt.matmul(pq6[0:1, 0:512], onesc[:],
                              sqb[:, dt * 512:(dt + 1) * 512], start=(dt == 0), stop=(dt == 7))
            m.then_inc(sT.h, sT.bump())

        @block.vector
        def _(v):
            v.wait_ge(sT.h, sT.n)
            v.tensor_scalar(rows[:, MU:MU + 512], pq7[0:1, 0:512], 1.0 / D, None, ALU.mult)
            v.tensor_scalar(rows[:, TMP:TMP + 512], pq6[0:1, 0:512], 1.0 / D, None, ALU.mult)
            v.tensor_tensor(asc[0:1, 0:512], rows[:, MU:MU + 512], rows[:, MU:MU + 512], ALU.mult)
            v.tensor_tensor(rows[:, TMP:TMP + 512], rows[:, TMP:TMP + 512], asc[0:1, 0:512],
                            ALU.subtract).then_inc(sV.h, sV.bump())

        @block.scalar
        def _(s):
            s.wait_ge(sV.h, sV.n)
            s.activation(rows[:, TMP:TMP + 512], rows[:, TMP:TMP + 512], AF.Ln, bias=cst[0:1, 0:1]).then_inc(sS.h, sS.bump())
            s.wait_ge(sS.h, sS.n)
            s.activation(rows[:, TMP:TMP + 512], rows[:, TMP:TMP + 512], AF.Exp,
                         scale=-0.5).then_inc(sS.h, sS.bump())

        @block.vector
        def _(v):
            v.wait_ge(sS.h, sS.n)
            v.tensor_copy(rbf[:, 0:512], rows[:, TMP:TMP + 512])
            v.tensor_tensor(rbf[:, 512:1024], rows[:, MU:MU + 512], rows[:, TMP:TMP + 512],
                            ALU.mult).then_inc(sV.h, sV.bump())

        @block.tensor
        def _(tt):
            tt.wait_ge(sV.h, sV.n)
            tt.matmul(pq3[:, 0:512], onesr[:], rbf[:, 0:512], start=True, stop=True)
            m = tt.matmul(pq4[:, 0:512], onesr[:], rbf[:, 512:1024], start=True, stop=True)
            m.then_inc(sT.h, sT.bump())

        @block.vector
        def _(v):
            v.wait_ge(sT.h, sT.n)
            for dt in range(8):
                sl = slice(dt * 512, (dt + 1) * 512)
                v.tensor_tensor(lnx[:, sl], xbf[:, sl], pq3[:, 0:512], ALU.mult)
                v.tensor_tensor(lnx[:, sl], lnx[:, sl], pq4[:, 0:512], ALU.subtract)
            v.nop().then_inc(sV.h, sV.bump())

        ctx.barrier()

        for pi, (wn, dstf_) in enumerate((("wq", qf), ("wk", kf))):
            for od in range(4):
                @block.tensor
                def _(tt, wn=wn, od=od, pi=pi):
                    if pi == 0 and od < 2:
                        tt.wait_ge(sV.h, sV.n)
                    else:
                        tt.wait_ge(sV.h, sV.n - 1)
                    ps = pq1 if od % 2 == 0 else pq2
                    for g in range(8):
                        m = tt.matmul(ps[:, 0:512], W[wn][:, g * 512 + od * 128: g * 512 + od * 128 + 128],
                                      lnx[:, g * 512:(g + 1) * 512], start=(g == 0), stop=(g == 7))
                    m.then_inc(sT.h, sT.bump())

                @block.vector
                def _(v, wn=wn, od=od, pi=pi):
                    v.wait_ge(sT.h, sT.n)
                    ps = pq1 if od % 2 == 0 else pq2
                    dstf = qf if wn == "wq" else kf
                    v.tensor_scalar(dstf[:, od * 512:(od + 1) * 512], ps[:, 0:512], 0.125,
                                    bqkc[:, pi * 4 + od: pi * 4 + od + 1],
                                    ALU.mult, ALU.add).then_inc(sV.h, sV.bump())

        for pi, (wn, dstt, boff) in enumerate((("wv", vtm, 0), ("wb", btm, 512), ("wa", atm, 1024))):
            for tc in range(4):
                @block.tensor
                def _(tt, wn=wn, tc=tc, pi=pi):
                    tt.wait_ge(sV.h, sV.n - (0 if (pi == 0 and tc < 2) else 1))
                    ps = pq1 if tc % 2 == 0 else pq2
                    for g in range(8):
                        m = tt.matmul(ps[:, 0:512], lnx[:, g * 512 + tc * 128: g * 512 + tc * 128 + 128],
                                      W[wn][:, g * 512:(g + 1) * 512], start=(g == 0), stop=(g == 7))
                    m.then_inc(sT.h, sT.bump())

                @block.vector
                def _(v, wn=wn, tc=tc, pi=pi, dstt=dstt, boff=boff):
                    v.wait_ge(sT.h, sT.n)
                    ps = pq1 if tc % 2 == 0 else pq2
                    v.tensor_tensor(dstt[:, tc * 512:(tc + 1) * 512], ps[:, 0:512],
                                    bvab[:, boff:boff + 512], ALU.add).then_inc(sV.h, sV.bump())

        ctx.barrier()

        @block.scalar
        def _(s):
            s.activation(asc[:], btm[:], AF.Abs).then_inc(sS.h, sS.bump())
            s.wait_ge(sS.h, sS.n)
            s.activation(asc[:], asc[:], AF.Exp, scale=-1.0).then_inc(sS.h, sS.bump())
            s.wait_ge(sS.h, sS.n)
            s.activation(asc[:], asc[:], AF.Ln, bias=cst[:, 1:2]).then_inc(sS.h, sS.bump())

        @block.vector
        def _(v):
            v.wait_ge(sS.h, sS.n)
            v.tensor_scalar(btm[:], btm[:], 0.0, None, ALU.max)
            v.tensor_tensor(btm[:], btm[:], asc[:], ALU.add)
            v.tensor_tensor(vb[:], vtm[:], btm[:], ALU.mult)
            for tc in range(4):
                for h in range(8):
                    v.reduce_sum(bms[:, tc * 8 + h: tc * 8 + h + 1],
                                 btm[:, tc * 512 + h * 64: tc * 512 + (h + 1) * 64],
                                 axis=mybir.AxisListType.X)
            v.tensor_scalar(atm[:], atm[:], 10.0, -10.0, ALU.min, ALU.max).then_inc(sV.h, sV.bump())

        @block.scalar
        def _(s):
            s.wait_ge(sV.h, sV.n)
            s.activation(asc[:], atm[:], AF.Abs).then_inc(sS.h, sS.bump())
            s.wait_ge(sS.h, sS.n)
            s.activation(asc[:], asc[:], AF.Exp, scale=-1.0).then_inc(sS.h, sS.bump())
            s.wait_ge(sS.h, sS.n)
            s.activation(asc[:], asc[:], AF.Ln, bias=cst[:, 1:2]).then_inc(sS.h, sS.bump())

        @block.vector
        def _(v):
            v.wait_ge(sS.h, sS.n)
            v.tensor_scalar(atm[:], atm[:], 0.0, None, ALU.min)
            v.tensor_tensor(atm[:], atm[:], asc[:], ALU.subtract)
            for tc in range(4):
                for h in range(8):
                    r = v.reduce_sum(lsum[:, tc * 8 + h: tc * 8 + h + 1],
                                     atm[:, tc * 512 + h * 64: tc * 512 + (h + 1) * 64],
                                     axis=mybir.AxisListType.X)
                    if tc == 3 and h == 7:
                        r.then_inc(sV.h, sV.bump())
            v.wait_ge(sV.h, sV.n)
            v.tensor_scalar(lam[:], lsum[:], 1.0 / 64, None, ALU.mult).then_inc(sV.h, sV.bump())

        ctx.barrier()

        if "dbg" in io:
            @block.gpsimd
            def _(g, blk=blk):
                if blk == 0:
                    g.dma_start(out=io["dbg2"][:, 0:512], in_=atm[:, 1536:2048]).then_inc(sO.h, sO.bump(16))
                    g.dma_start(out=io["dbg2"][:, 512:1024], in_=asc[:, 1536:2048]).then_inc(sO.h, sO.bump(16))
                    g.dma_start(out=io["dbg2"][:, 1024:1056], in_=lam[:]).then_inc(sO.h, sO.bump(16))
                    g.dma_start(out=io["dbg2"][:, 1056:1088], in_=lsum[:]).then_inc(sO.h, sO.bump(16))
                    g.wait_ge(sO.h, sO.n)

            ctx.barrier()

        for c in range(4):
            @block.tensor
            def _(tt, c=c):
                m = tt.matmul(pq7[:, 0:8], trilc[:], lam[:, c * 8:(c + 1) * 8], start=True, stop=True)
                m.then_inc(sT.h, sT.bump())

            @block.vector
            def _(v, c=c):
                v.wait_ge(sT.h, sT.n)
                v.tensor_copy(mcol[:, 0:8], pq7[:, 0:8])
                v.tensor_scalar(mcol[:, 8:16], bms[:, c * 8:(c + 1) * 8], 1.0 / 64, None,
                                ALU.mult).then_inc(sV.h, sV.bump())

            @block.scalar
            def _(s, c=c):
                s.wait_ge(sV.h, sV.n)
                s.activation(curvec[:], mcol[:, 0:8], AF.Exp).then_inc(sS.h, sS.bump())

            @block.tensor
            def _(tt, c=c):
                tt.wait_ge(sV.h, sV.n)
                m = tt.transpose(pq7[0:16, 256:384], mcol[:], idf[:])
                m.then_inc(sT.h, sT.bump())

            @block.vector
            def _(v, c=c):
                v.wait_ge(sT.h, sT.n)
                v.tensor_copy(mbt16[:], pq7[0:16, 256:384]).then_inc(sV.h, sV.bump())

            @block.gpsimd
            def _(g, c=c):
                g.wait_ge(sV.h, sV.n)
                g.dma_start(out=bass.AP(stage, 0, [[2048, 1], [128, 16], [1, 128]]),
                            in_=bass.AP(mbt16, 0, [[128, 16], [1, 128]])).then_inc(sG.h, sG.bump(16))
                g.wait_ge(sG.h, sG.n)

            @block.vector
            def _(v, c=c):
                v.wait_ge(sG.h, sG.n)
                v.tensor_copy(stg2b[:, 0:1024], stage[:, 0:1024])
                v.tensor_tensor(stg2b[:, 1024:2048], stage[:, 0:1024], stg2b[:, 0:1024], ALU.subtract)
                v.tensor_scalar(stg2b[:, 2048:3072], stg2b[:, 0:1024], -1.0, None, ALU.mult)
                v.tensor_scalar(stg2b[:, 3072:4096], stg2b[:, 1024:2048], -1.0, None, ALU.mult)
                v.tensor_copy(bmb[:], stage[:, 1024:2048]).then_inc(sV.h, sV.bump())

            @block.scalar
            def _(s, c=c):
                s.wait_ge(sG.h, sG.n)
                s.activation(cfb[:], stage[:, 0:1024], AF.Exp).then_inc(sS.h, sS.bump())

            @block.gpsimd
            def _(g, c=c):
                g.wait_ge(sV.h, sV.n)
                g.wait_ge(sS.h, sS.n)
                g.dma_start(out=bass.AP(lhsA, 0, [[1024, 2], [1, 1024]]),
                            in_=bass.AP(stg2b, 2048, [[4096, 1], [1024, 2], [1, 1024]])).then_inc(sG.h, sG.bump(16))
                g.dma_start(out=bass.AP(rhsA, 2 * 1024, [[1024, 2], [1, 1024]]),
                            in_=bass.AP(stg2b, 0, [[4096, 1], [1024, 2], [1, 1024]])).then_inc(sG.h, sG.bump(16))
                g.dma_start(out=bass.AP(c8, 0, [[128, 8], [1, 128]]),
                            in_=bass.AP(cfb, 0, [[1024, 1], [128, 8], [1, 128]])).then_inc(sG.h, sG.bump(16))
                g.dma_start(out=bass.AP(b8, 0, [[128, 8], [1, 128]]),
                            in_=bass.AP(bmb, 0, [[1024, 1], [128, 8], [1, 128]])).then_inc(sG.h, sG.bump(16))
                g.wait_ge(sG.h, sG.n)

            @block.tensor
            def _(tt, c=c):
                tt.wait_ge(sG.h, sG.n)
                tt.matmul(pq6[:, 0:512], c8[:], sel8[:], start=True, stop=True)
                tt.matmul(pq5[:, 0:512], b8[:], sel8[:], start=True, stop=True)
                for p in range(4):
                    for j, hh in enumerate((2 * p, 2 * p + 1)):
                        m = tt.matmul(pq7[:, 16 + p:17 + p], indr[0:1, j * 128:(j + 1) * 128],
                                      cfb[0:1, hh * 128 + 127: hh * 128 + 128],
                                      start=(j == 0), stop=(j == 1))
                m.then_inc(sT.h, sT.bump())

            @block.vector
            def _(v, c=c):
                v.wait_ge(sT.h, sT.n)
                v.tensor_copy(ctm[:], pq6[:, 0:512])
                v.tensor_tensor(vbm[:], vtm[:, c * 512:(c + 1) * 512], pq5[:, 0:512], ALU.mult)
                v.tensor_copy(asumc[:], pq7[:, 16:20]).then_inc(sV.h, sV.bump())

            if "dbg" in io:
                @block.gpsimd
                def _(g, c=c, blk=blk):
                    if blk == 0:
                        g.dma_start(out=io["dbg"][:, c * 1024: c * 1024 + 512], in_=ctm[:]).then_inc(sO.h, sO.bump(16))
                        g.dma_start(out=io["dbg"][:, c * 1024 + 512: c * 1024 + 520], in_=curvec[:]).then_inc(sO.h, sO.bump(16))
                        g.dma_start(out=io["dbg"][:, c * 1024 + 520: c * 1024 + 524], in_=asumc[:]).then_inc(sO.h, sO.bump(16))
                        g.dma_start(out=io["dbg"][0:1, c * 1024 + 524: c * 1024 + 524 + 256],
                                    in_=stage[0:1, 896:1152]).then_inc(sO.h, sO.bump(16))
                        g.wait_ge(sO.h, sO.n)

            ctx.barrier()

            for qd in range(2):
                @block.tensor
                def _(tt, c=c, qd=qd):
                    tt.wait_ge(sV.h, sV.n)
                    for h4 in range(4):
                        h = qd * 4 + h4
                        tt.matmul(pq1[:, h4 * 128:(h4 + 1) * 128], lhsA[:, h * 128:(h + 1) * 128],
                                  rhsA[:, h * 128:(h + 1) * 128], start=True, stop=True)
                        rt, po = h // 2, (h % 2) * 64
                        m = tt.matmul(pq2[:, h4 * 128:(h4 + 1) * 128],
                                      kf[po:po + 64, rt * 512 + c * 128: rt * 512 + c * 128 + 128],
                                      qf[po:po + 64, rt * 512 + c * 128: rt * 512 + c * 128 + 128],
                                      start=True, stop=True)
                    m.then_inc(sT.h, sT.bump())

                @block.vector
                def _(v, c=c, qd=qd):
                    v.wait_ge(sT.h, sT.n)
                    v.tensor_tensor(dec[:], pq1[:, 0:512], mneg[:], ALU.add).then_inc(sV.h, sV.bump())

                @block.scalar
                def _(s, c=c, qd=qd):
                    s.wait_ge(sV.h, sV.n)
                    s.activation(dec[:], dec[:], AF.Exp).then_inc(sS.h, sS.bump())

                @block.vector
                def _(v, c=c, qd=qd):
                    v.wait_ge(sS.h, sS.n)
                    if qd == 1:
                        v.wait_ge(sT.h, sT.n)
                    v.tensor_tensor(st4[:], pq2[:, 0:512], dec[:], ALU.mult).then_inc(sV.h, sV.bump())

                @block.tensor
                def _(tt, c=c, qd=qd):
                    tt.wait_ge(sV.h, sV.n)
                    for h4 in range(4):
                        h = qd * 4 + h4
                        tt.matmul(pq3[:, h4 * 64:(h4 + 1) * 64], st4[:, h4 * 128:(h4 + 1) * 128],
                                  vb[:, c * 512 + h * 64: c * 512 + (h + 1) * 64],
                                  start=True, stop=True)
                        rt, po = h // 2, (h % 2) * 64
                        m = tt.matmul(pq4[:, h4 * 64:(h4 + 1) * 64],
                                      qf[po:po + 64, rt * 512 + c * 128: rt * 512 + c * 128 + 128],
                                      stpb[po:po + 64, rt * 64:(rt + 1) * 64],
                                      start=True, stop=True)
                    m.then_inc(sT.h, sT.bump())

                @block.vector
                def _(v, c=c, qd=qd):
                    v.wait_ge(sT.h, sT.n)
                    v.tensor_tensor(memc[:], pq4[:, 0:256], ctm[:, qd * 256:(qd + 1) * 256], ALU.mult)
                    v.tensor_tensor(ysb[:, qd * 256:(qd + 1) * 256], pq3[:, 0:256], memc[:],
                                    ALU.add).then_inc(sV.h, sV.bump())

                ctx.barrier()

            @block.tensor
            def _(tt, c=c):
                tt.wait_ge(sV.h, sV.n)
                for p in range(4):
                    m = tt.transpose(pqb[:, p * 128:(p + 1) * 128],
                                     kf[:, p * 512 + c * 128: p * 512 + c * 128 + 128], idb[:])
                m.then_inc(sT.h, sT.bump())

            @block.vector
            def _(v, c=c):
                v.wait_ge(sT.h, sT.n)
                v.tensor_copy(ktm[:], pqb[:, 0:512]).then_inc(sV.h, sV.bump())

            @block.tensor
            def _(tt, c=c):
                tt.wait_ge(sV.h, sV.n)
                for h in range(8):
                    po = (h % 2) * 64
                    m = tt.matmul(pq4[po:po + 64, (h // 2) * 64:(h // 2 + 1) * 64],
                                  ktm[:, h * 64:(h + 1) * 64], vbm[:, h * 64:(h + 1) * 64],
                                  start=True, stop=True)
                m.then_inc(sT.h, sT.bump())

            @block.vector
            def _(v, c=c):
                v.wait_ge(sT.h, sT.n)
                for p in range(4):
                    v.scalar_tensor_tensor(stpf[:, p * 64:(p + 1) * 64], stpf[:, p * 64:(p + 1) * 64],
                                           asumc[:, p:p + 1], pq4[:, p * 64:(p + 1) * 64],
                                           ALU.mult, ALU.add)
                v.tensor_copy(stpb[:], stpf[:]).then_inc(sV.h, sV.bump())

            ctx.barrier()

            @block.tensor
            def _(tt, c=c):
                tt.wait_ge(sV.h, sV.n)
                for p in range(4):
                    m = tt.transpose(pq5[:, p * 128:(p + 1) * 128], ysb[:, p * 128:(p + 1) * 128], idf[:])
                m.then_inc(sT.h, sT.bump())

            @block.vector
            def _(v, c=c):
                v.wait_ge(sT.h, sT.n)
                for p in range(4):
                    v.tensor_tensor(x1c[:, p * 128:(p + 1) * 128], pq5[:, p * 128:(p + 1) * 128],
                                    xown[:, p * 512 + c * 128: p * 512 + c * 128 + 128], ALU.add)
                v.nop().then_inc(sV.h, sV.bump())

            @block.sync
            def _(sync, c=c, tok0=tok0):
                sync.wait_ge(sV.h, sV.n)
                dst = bass.AP(io["xout1"].tensor, tok0 + c * 128, [[T, 128], [128 * T, 4], [1, 128]])
                src = bass.AP(x1c, 0, [[512, 128], [128, 4], [1, 128]])
                sync.dma_start(out=dst, in_=src).then_inc(sO.h, sO.bump(16))
                sync.wait_ge(sO.h, sO.n)

            ctx.barrier()


L0ATTN_CONSTS = {}


def l0_attn_consts():
    if L0ATTN_CONSTS:
        return L0ATTN_CONSTS
    i_idx = np.arange(128)
    ctril = (i_idx[:, None] <= i_idx[None, :]).astype(BF16NP)
    mask1 = np.where(i_idx[:, None] <= i_idx[None, :], 0.0, -1e9).astype(np.float32)
    cmask = np.tile(mask1, (1, 4))
    sel = np.zeros((8, 512), BF16NP)
    for h in range(8):
        sel[h, h * 64:(h + 1) * 64] = 1
    cidf = np.eye(128, dtype=np.float32)
    cidb = np.eye(128).astype(BF16NP)
    cind = np.zeros((1, 256), BF16NP)
    cind[0, 0:64] = 1
    cind[0, 128 + 64:256] = 1
    cones2 = np.ones((1, 2048), BF16NP)
    L0ATTN_CONSTS.update(dict(ctril=ctril, cmask=cmask, csel=sel, cidf=cidf, cidb=cidb,
                              cind=cind, cones2=cones2))
    return L0ATTN_CONSTS


L0ATTN_INPUT_SPECS = [
    ("xT", [D, T], F32),
    ("xownT", [512, T], F32),
    ("wq", [1024, 512], F32), ("wk", [1024, 512], F32), ("wv", [1024, 512], F32),
    ("wb", [1024, 512], F32), ("wa", [1024, 512], F32),
    ("bqk_col", [128, 8], F32),
    ("bva_row", [1, 1536], F32),
    ("ones_row_f", [1, 128], F32),
    ("ctril", [128, 128], BF16), ("cmask", [128, 512], F32), ("csel", [8, 512], BF16),
    ("cidf", [128, 128], F32), ("cidb", [128, 128], BF16), ("cind", [1, 256], BF16),
    ("cones2", [1, 2048], BF16),
    ("ones_bf", [128, 1], BF16), ("ones_row_bf", [1, 128], BF16), ("consts", [128, 8], F32),
]


def host_l0_attn_inputs(core, x_full, inp):
    b, hh = core // 2, core % 2
    o0 = hh * 512
    cst = l0_attn_consts()
    d = dict(
        xT=np.ascontiguousarray(x_full[b].T),
        xownT=np.ascontiguousarray(x_full[b].T[o0:o0 + 512]),
        bqk_col=np.ascontiguousarray(np.concatenate(
            [np.asarray(inp["L0_attn_bq"], np.float32)[o0:o0 + 512].reshape(4, 128).T,
             np.asarray(inp["L0_attn_bk"], np.float32)[o0:o0 + 512].reshape(4, 128).T], axis=1)),
        bva_row=np.ascontiguousarray(np.concatenate(
            [np.asarray(inp["L0_attn_bv"], np.float32)[o0:o0 + 512],
             np.asarray(inp["L0_attn_bbeta"], np.float32)[o0:o0 + 512],
             np.asarray(inp["L0_attn_balpha"], np.float32)[o0:o0 + 512]]).reshape(1, -1)),
        ones_row_f=np.ones((1, 128), np.float32),
        ones_bf=np.ones((128, 1), BF16NP),
        ones_row_bf=np.ones((1, 128), BF16NP),
        consts=np.tile(np.array([[1e-5, 1.0, 0, 0, 0, 0, 0, 0]], np.float32), (128, 1)),
    )
    for nm, key in (("wq", "Wq"), ("wk", "Wk"), ("wv", "Wv"), ("wb", "Wbeta"), ("wa", "Walpha")):
        Wm = np.asarray(inp[f"L0_attn_{key}"], np.float32)
        d[nm] = np.ascontiguousarray(Wm[o0:o0 + 512, :].T)
    d.update({k: np.ascontiguousarray(v) for k, v in cst.items()})
    return d


def build_l0_attn_launch():
    nc = bass.Bass("TRN2", target_bir_lowering=False, debug=False)
    io = {}
    for name, shape, dt in L0ATTN_INPUT_SPECS:
        io[name] = nc.dram_tensor(name, shape, dt, kind="ExternalInput").ap()
    io["xout1"] = nc.dram_tensor("xout1", [512, T], F32, kind="ExternalOutput").ap()
    io["dbg"] = nc.dram_tensor("dbg", [128, 4096], F32, kind="ExternalOutput").ap()
    io["dbg2"] = nc.dram_tensor("dbg2", [128, 1088], F32, kind="ExternalOutput").ap()
    with contextlib.ExitStack() as stack:
        block = stack.enter_context(nc.Block())
        ctx = Ctx(nc, block, stack)
        emit_l0_attn(ctx, io)
    return nc



def emit_l1_attn(ctx, io, upto=99):
    nc, block = ctx.nc, ctx.block
    NB = T // 512
    sL = ctx.sem("bL"); sT = ctx.sem("bT"); sV = ctx.sem("bV")
    sS = ctx.sem("bS"); sO = ctx.sem("bO"); sG = ctx.sem("bG")
    sb = lambda n, sh, dt: ctx.stack.enter_context(nc.sbuf_tensor(n, sh, dt))
    ps_ = lambda n, sh, dt=F32: ctx.stack.enter_context(nc.psum_tensor(n, sh, dt))

    ibuf = sb("b_ibuf", [128, 512], F32)
    cbuf = sb("b_cbuf", [128, 1024], F32)
    Tt = sb("b_T", [128, 8192], BF16)
    Arev = sb("b_Arev", [128, 8192], BF16)
    CAt = sb("b_CA", [128, 8192], BF16)
    A128 = sb("b_A128", [128, 64], F32)
    U8s = sb("b_U8s", [128, 64 * 256], BF16)
    u_all = sb("b_uall", [128, 16 * 64], BF16)
    u_pad = sb("b_upad", [16, 4 * 136], F32)
    c_hist = sb("b_chist", [128, 4160], F32)
    r_all = sb("b_rall", [128, 4160], F32)
    c2 = sb("b_c2", [128, 8192], BF16)
    yct = sb("b_yct", [128, 8192], BF16)
    xbf = sb("b_xbf", [128, 8 * 512], BF16)
    xown = sb("b_xown", [128, 4 * 512], F32)
    lnx = sb("b_lnx", [128, 8 * 512], BF16)
    sqb = lnx
    tscr = sb("b_tscr", [128, 4096], F32)
    rows = sb("b_rows", [1, 1024], F32)
    rbf = sb("b_rbf", [1, 1024], BF16)
    BwT = sb("b_BwT", [128, 8 * 16], BF16)
    Dcol = sb("b_Dcol", [128, 4], F32)
    idb = sb("b_idb", [128, 128], BF16)
    onesc = sb("b_onesc", [128, 1], BF16)
    onesr = sb("b_onesr", [1, 128], BF16)
    cst = sb("b_cst", [128, 8], F32)

    pq1 = ps_("b_pq1", [128, 512])
    pq2 = ps_("b_pq2", [128, 512])
    pq3 = ps_("b_pq3", [128, 512])
    pq4 = ps_("b_pq4", [128, 512])
    pqb = ps_("b_pqb", [128, 2048], BF16)
    MU, TMP = 0, 512

    @block.sync
    def _(sync):
        for key, dst in (("consts", cst), ("ones_bf", onesc), ("ones_row_bf", onesr),
                         ("Dcol", Dcol), ("cidb", idb)):
            sync.dma_start(out=dst[:], in_=io[key]).then_inc(sL.h, sL.bump(16))
        sync.wait_ge(sL.h, sL.n)

    @block.gpsimd
    def _(g):
        src = bass.AP(io["BwT_in"].tensor, 0, [[16, 128], [128 * 16, 8], [1, 16]])
        dst = bass.AP(BwT, 0, [[128, 128], [16, 8], [1, 16]])
        g.dma_start(out=dst, in_=src).then_inc(sG.h, sG.bump(16))
        g.wait_ge(sG.h, sG.n)

    for nm, tab, logA_key, tmat_key, c_key in (
            ("ti", Tt, "logA_intra", "tmat_intra", "C_intra"),
            ("tr", Arev, "logA_rev", "tmat_rev", None),
            ("tc", CAt, "logA_ca", "tmat_ca", "C_ca")):
        @block.gpsimd
        def _(g, tab=tab, tmat_key=tmat_key, c_key=c_key):
            g.wait_ge(sV.h, sV.n)
            g.dma_start(out=tab[:], in_=io[tmat_key]).then_inc(sG.h, sG.bump(16))
            if c_key is not None:
                g.dma_start(out=yct[:], in_=io[c_key]).then_inc(sG.h, sG.bump(16))
            g.wait_ge(sG.h, sG.n)

        for hf in range(2):
            hsl = slice(hf * 4096, (hf + 1) * 4096)

            @block.sync
            def _(sync, logA_key=logA_key, hsl=hsl):
                sync.wait_ge(sV.h, sV.n)
                sync.dma_start(out=tscr[:], in_=io[logA_key][:, hsl]).then_inc(sL.h, sL.bump(16))
                sync.wait_ge(sL.h, sL.n)

            @block.scalar
            def _(s):
                s.wait_ge(sL.h, sL.n)
                s.wait_ge(sG.h, sG.n)
                s.activation(tscr[:], tscr[:], AF.Exp).then_inc(sS.h, sS.bump())

            @block.vector
            def _(v, tab=tab, hsl=hsl):
                v.wait_ge(sS.h, sS.n)
                v.tensor_tensor(tscr[:], tscr[:], tab[:, hsl], ALU.mult).then_inc(sV.h, sV.bump())

            @block.scalar
            def _(s):
                s.wait_ge(sV.h, sV.n)
                s.activation(tscr[:], tscr[:], AF.Exp, scale=-1.0).then_inc(sS.h, sS.bump())

            @block.vector
            def _(v, tab=tab, c_key=c_key, hsl=hsl):
                v.wait_ge(sS.h, sS.n)
                if c_key is None:
                    v.tensor_copy(tab[:, hsl], tscr[:]).then_inc(sV.h, sV.bump())
                else:
                    v.tensor_tensor(tab[:, hsl], tscr[:], yct[:, hsl], ALU.mult).then_inc(sV.h, sV.bump())

    @block.sync
    def _(sync):
        sync.wait_ge(sV.h, sV.n)
        sync.dma_start(out=tscr[:, 0:64], in_=io["logA_scan"]).then_inc(sL.h, sL.bump(16))
        sync.wait_ge(sL.h, sL.n)

    @block.scalar
    def _(s):
        s.wait_ge(sL.h, sL.n)
        s.activation(tscr[:, 0:64], tscr[:, 0:64], AF.Exp).then_inc(sS.h, sS.bump())

    @block.vector
    def _(v):
        v.wait_ge(sS.h, sS.n)
        v.tensor_scalar(tscr[:, 64:128], tscr[:, 0:64], -128.0, None, ALU.mult).then_inc(sV.h, sV.bump())

    @block.scalar
    def _(s):
        s.wait_ge(sV.h, sV.n)
        s.activation(A128[:], tscr[:, 64:128], AF.Exp).then_inc(sS.h, sS.bump())

    @block.vector
    def _(v):
        v.memset(U8s[:], 0.0)
        v.memset(c2[:], 0.0)
        v.memset(u_pad[:], 0.0)
        v.memset(c_hist[:, 0:1040], 0.0)
        v.memset(c_hist[:, 1040:4160], 0.0)

    ctx.barrier()
    if upto < 1:
        return

    for blk in range(NB):
        tok0 = blk * 512

        @block.sync
        def _(sync, tok0=tok0):
            src = bass.AP(io["x2ownT"].tensor, tok0, [[T, 128], [128 * T, 4], [1, 512]])
            dst = bass.AP(xown, 0, [[4 * 512, 128], [512, 4], [1, 512]])
            sync.dma_start(out=dst, in_=src).then_inc(sL.h, sL.bump(16))
            sync.wait_ge(sL.h, sL.n)

        @block.gpsimd
        def _(g, tok0=tok0):
            src = bass.AP(io["x2T"].tensor, tok0, [[T, 128], [128 * T, 8], [1, 512]])
            dst = bass.AP(xbf, 0, [[8 * 512, 128], [512, 8], [1, 512]])
            g.dma_start(out=dst, in_=src).then_inc(sG.h, sG.bump(16))
            g.wait_ge(sG.h, sG.n)

        ctx.barrier()

        @block.scalar
        def _(s):
            s.activation(sqb[:], xbf[:], AF.Square).then_inc(sS.h, sS.bump())

        @block.tensor
        def _(tt):
            for dt in range(8):
                tt.matmul(pq1[0:1, 0:512], onesc[:], xbf[:, dt * 512:(dt + 1) * 512],
                          start=(dt == 0), stop=(dt == 7))
            tt.wait_ge(sS.h, sS.n)
            for dt in range(8):
                m = tt.matmul(pq2[0:1, 0:512], onesc[:], sqb[:, dt * 512:(dt + 1) * 512],
                              start=(dt == 0), stop=(dt == 7))
            m.then_inc(sT.h, sT.bump())

        @block.vector
        def _(v):
            v.wait_ge(sT.h, sT.n)
            v.tensor_scalar(rows[:, MU:MU + 512], pq1[0:1, 0:512], 1.0 / D, None, ALU.mult)
            v.tensor_scalar(rows[:, TMP:TMP + 512], pq2[0:1, 0:512], 1.0 / D, None, ALU.mult)
            v.tensor_tensor(tscr[0:1, 0:512], rows[:, MU:MU + 512], rows[:, MU:MU + 512], ALU.mult)
            v.tensor_tensor(rows[:, TMP:TMP + 512], rows[:, TMP:TMP + 512], tscr[0:1, 0:512],
                            ALU.subtract).then_inc(sV.h, sV.bump())

        @block.scalar
        def _(s):
            s.wait_ge(sV.h, sV.n)
            s.activation(rows[:, TMP:TMP + 512], rows[:, TMP:TMP + 512], AF.Ln, bias=cst[0:1, 0:1]).then_inc(sS.h, sS.bump())
            s.wait_ge(sS.h, sS.n)
            s.activation(rows[:, TMP:TMP + 512], rows[:, TMP:TMP + 512], AF.Exp,
                         scale=-0.5).then_inc(sS.h, sS.bump())

        @block.vector
        def _(v):
            v.wait_ge(sS.h, sS.n)
            v.tensor_copy(rbf[:, 0:512], rows[:, TMP:TMP + 512])
            v.tensor_tensor(rbf[:, 512:1024], rows[:, MU:MU + 512], rows[:, TMP:TMP + 512],
                            ALU.mult).then_inc(sV.h, sV.bump())

        @block.tensor
        def _(tt):
            tt.wait_ge(sV.h, sV.n)
            tt.matmul(pq3[:, 0:512], onesr[:], rbf[:, 0:512], start=True, stop=True)
            m = tt.matmul(pq4[:, 0:512], onesr[:], rbf[:, 512:1024], start=True, stop=True)
            m.then_inc(sT.h, sT.bump())

        @block.vector
        def _(v, blk=blk):
            v.wait_ge(sT.h, sT.n)
            for dt in range(8):
                sl = slice(dt * 512, (dt + 1) * 512)
                v.tensor_tensor(lnx[:, sl], xbf[:, sl], pq3[:, 0:512], ALU.mult)
                v.tensor_tensor(lnx[:, sl], lnx[:, sl], pq4[:, 0:512], ALU.subtract)
            for dc in range(4):
                sl = slice(dc * 512, (dc + 1) * 512)
                v.tensor_tensor(tscr[:, 0:512], xown[:, sl], pq3[:, 0:512], ALU.mult)
                v.tensor_tensor(tscr[:, 0:512], tscr[:, 0:512], pq4[:, 0:512], ALU.subtract)
                v.tensor_scalar(tscr[:, 0:512], tscr[:, 0:512], Dcol[:, dc:dc + 1], None, ALU.mult)
                m = v.tensor_tensor(xown[:, sl], xown[:, sl], tscr[:, 0:512], ALU.add)
            m.then_inc(sV.h, sV.bump())

        @block.sync
        def _(sync, blk=blk):
            sync.wait_ge(sV.h, sV.n)
            dst = bass.AP(io["xout3"].tensor, blk * 512, [[T, 128], [128 * T, 4], [1, 512]])
            srcx = bass.AP(xown, 0, [[4 * 512, 128], [512, 4], [1, 512]])
            sync.dma_start(out=dst, in_=srcx).then_inc(sO.h, sO.bump(16))
            sync.wait_ge(sO.h, sO.n)

        ctx.barrier()

        @block.tensor
        def _(tt, blk=blk):
            for dt in range(8):
                m = tt.matmul(pq1[0:16, 0:512], BwT[:, dt * 16:(dt + 1) * 16],
                              lnx[:, dt * 512:(dt + 1) * 512], start=(dt == 0), stop=(dt == 7))
            m.then_inc(sT.h, sT.bump())

        @block.vector
        def _(v, blk=blk):
            v.wait_ge(sT.h, sT.n)
            dst = bass.AP(u_pad, 8, [[4 * 136, 16], [136, 4], [1, 128]])
            srcp = bass.AP(pq1, 0, [[512, 16], [128, 4], [1, 128]])
            m = v.tensor_copy(dst, srcp)
            m.then_inc(sV.h, sV.bump())

        @block.tensor
        def _(tt, blk=blk):
            tt.wait_ge(sV.h, sV.n)
            for ci in range(4):
                ps = (pq1, pq2, pq3, pq4)[ci]
                for dt in range(8):
                    m = tt.matmul(ps[:, 0:16],
                                  lnx[:, dt * 512 + ci * 128: dt * 512 + ci * 128 + 128],
                                  BwT[:, dt * 16:(dt + 1) * 16],
                                  start=(dt == 0), stop=(dt == 7))
            m.then_inc(sT.h, sT.bump())

        @block.vector
        def _(v, blk=blk):
            v.wait_ge(sT.h, sT.n)
            for ci in range(4):
                ps = (pq1, pq2, pq3, pq4)[ci]
                dst2 = bass.AP(u_all, blk * 4 + ci, [[16 * 64, 128], [64, 16]])
                m = v.tensor_copy(dst2, ps[:, 0:16])
            m.then_inc(sV.h, sV.bump())

        @block.gpsimd
        def _(g, blk=blk):
            g.wait_ge(sV.h, sV.n)
            for ci in range(4):
                n = blk * 4 + ci
                for dlt in range(8):
                    src = bass.AP(u_pad, ci * 136 + 8 - dlt, [[4 * 136, 16], [1, 128]])
                    dst = bass.AP(U8s, dlt * 16384 + n * 256 + 120, [[8 * 16384, 16], [1, 128]])
                    g.dma_start(out=dst, in_=src).then_inc(sG.h, sG.bump(16))
            g.wait_ge(sG.h, sG.n)

        ctx.barrier()

    if upto < 2:
        return
    for s_ in range(16):
        for dc in range(4):
            k = s_ * 4 + dc
            @block.tensor
            def _(tt, s_=s_, dc=dc, k=k):
                if k >= 2:
                    tt.wait_ge(sV.h, sV.n - 1)
                ps = pq1 if k % 2 == 0 else pq2
                m = tt.matmul(ps[:, 0:64], Arev[:, s_ * 512 + dc * 128: s_ * 512 + dc * 128 + 128],
                              u_all[:, s_ * 64:(s_ + 1) * 64], start=True, stop=True)
                m.then_inc(sT.h, sT.bump())

            @block.vector
            def _(v, s_=s_, dc=dc, k=k):
                v.wait_ge(sT.h, sT.n)
                v.tensor_copy(r_all[:, dc * 1040 + s_ * 65: dc * 1040 + s_ * 65 + 64],
                              ps_sel(pq1, pq2, k)[:, 0:64]).then_inc(sV.h, sV.bump())

    ctx.barrier()

    if upto < 3:
        return
    for n in range(1, 64):
        @block.vector
        def _(v, n=n):
            cprev = bass.AP(c_hist, n - 1, [[4160, 128], [1040, 4], [65, 16]])
            cnew = bass.AP(c_hist, n, [[4160, 128], [1040, 4], [65, 16]])
            rprev = bass.AP(r_all, n - 1, [[4160, 128], [1040, 4], [65, 16]])
            a128v = bass.AP(A128, 0, [[64, 128], [16, 4], [1, 16]])
            tscrv = bass.AP(tscr, 0, [[4096, 128], [16, 4], [1, 16]])
            v.tensor_tensor(tscrv, cprev, a128v, ALU.mult)
            m = v.tensor_tensor(cnew, tscrv, rprev, ALU.add)
            m.then_inc(sV.h, sV.bump())
            v.wait_ge(sV.h, sV.n)

    ctx.barrier()

    if upto < 4:
        return
    for grp in range(16):
        for dc in range(4):
            @block.tensor
            def _(tt, grp=grp, dc=dc):
                if grp + dc > 0:
                    tt.wait_ge(sV.h, sV.n)
                for ci in range(4):
                    n = grp * 4 + ci
                    ps = (pq1, pq2, pq3, pq4)[ci]
                    for g in range(16):
                        lhs = Tt[:, g * 512 + dc * 128: g * 512 + dc * 128 + 128]
                        m = tt.matmul(ps[:, 0:128], lhs,
                                      U8s[:, n * 256 + 120 - 8 * g: n * 256 + 248 - 8 * g],
                                      start=(g == 0), stop=(g == 15))
                m.then_inc(sT.h, sT.bump())

            @block.vector
            def _(v, grp=grp, dc=dc):
                v.wait_ge(sT.h, sT.n)
                if upto >= 4.4 and grp + dc > 0:
                    v.wait_ge(sG.h, sG.n)
                for ci in range(4):
                    ps = (pq1, pq2, pq3, pq4)[ci]
                    m = v.tensor_copy(ibuf[:, ci * 128:(ci + 1) * 128], ps[:, 0:128])
                m.then_inc(sV.h, sV.bump())

            if upto >= 4.4:
                @block.gpsimd
                def _(g, grp=grp, dc=dc):
                    g.wait_ge(sV.h, sV.n)
                    dst = bass.AP(io["xout3"].tensor, dc * 128 * T + grp * 512, [[T, 128], [1, 512]])
                    g.dma_start(out=dst, in_=ibuf[:], accum_op=ALU.add if not io.get("_noaccum") else ALU.bypass).then_inc(sG.h, sG.bump(16))

    ctx.barrier()

    if upto < 5:
        return
    @block.sync
    def _(sync):
        dst = bass.AP(io["cdram"].tensor, 0, [[4160, 128], [1, 4160]])
        sync.dma_start(out=dst, in_=c_hist[:]).then_inc(sO.h, sO.bump(16))
        sync.wait_ge(sO.h, sO.n)

    ctx.barrier()

    for gh in range(4):
        @block.gpsimd
        def _(g, gh=gh):
            g.wait_ge(sV.h, sV.n)
            for dpp in range(8):
                srcd = bass.AP(io["cdram"].tensor, dpp * 4160 + gh * 1040,
                               [[65, 16], [4160 * 8, 16], [1, 64]])
                dst = bass.AP(c2, dpp * 16 * 8192 + dpp * 64, [[8192, 16], [512, 16], [1, 64]])
                g.dma_start(out=dst, in_=srcd).then_inc(sG.h, sG.bump(16))
            g.wait_ge(sG.h, sG.n)

        if upto < 5.4:
            continue
        for gl in range(16):
            @block.tensor
            def _(tt, gh=gh, gl=gl):
                if gl == 0:
                    tt.wait_ge(sG.h, sG.n)
                if gl >= 2:
                    tt.wait_ge(sV.h, sV.n - 1)
                gq = gh * 16 + gl
                ps = (pq1, pq2)[gl % 2]
                m = tt.matmul(ps[:, 0:512], CAt[:, gq * 128:(gq + 1) * 128],
                              c2[:, gl * 512:(gl + 1) * 512], start=True, stop=True)
                m.then_inc(sT.h, sT.bump())

            @block.vector
            def _(v, gh=gh, gl=gl):
                v.wait_ge(sT.h, sT.n)
                ps = (pq1, pq2)[gl % 2]
                dst = bass.AP(yct, gl * 8, [[8192, 128], [1, 8], [128, 64]])
                srcp = bass.AP(ps, 0, [[512, 128], [64, 8], [1, 64]])
                v.tensor_copy(dst, srcp).then_inc(sV.h, sV.bump())

        if upto < 5.6:
            continue
        for nb in range(8):
            for ni in range(8):
                n = nb * 8 + ni
                @block.tensor
                def _(tt, gh=gh, n=n, ni=ni, nb=nb):
                    if n == 0:
                        tt.wait_ge(sV.h, sV.n)
                    if n >= 2:
                        tt.wait_ge(sV.h, sV.n - 1)
                    pb = pqb[:, (n % 2) * 1024:(n % 2) * 1024 + 128]
                    m = tt.transpose(pb, yct[:, n * 128:(n + 1) * 128], idb[:])
                    m.then_inc(sT.h, sT.bump())

                @block.vector
                def _(v, gh=gh, n=n, ni=ni, nb=nb):
                    v.wait_ge(sT.h, sT.n)
                    if upto >= 5.8 and ni == 0:
                        v.wait_ge(sG.h, sG.n)
                    pb = pqb[:, (n % 2) * 1024:(n % 2) * 1024 + 128]
                    v.tensor_copy(cbuf[:, ni * 128:(ni + 1) * 128], pb).then_inc(sV.h, sV.bump())

            if upto >= 5.8:
                @block.gpsimd
                def _(g, gh=gh, nb=nb):
                    g.wait_ge(sV.h, sV.n)
                    dst = bass.AP(io["xout3"].tensor, gh * 128 * T + nb * 1024, [[T, 128], [1, 1024]])
                    g.dma_start(out=dst, in_=cbuf[:], accum_op=ALU.add).then_inc(sG.h, sG.bump(16))

    @block.gpsimd
    def _(g):
        g.wait_ge(sG.h, sG.n)

    ctx.barrier()


def ps_sel(a, b, k):
    return a if k % 2 == 0 else b


L1ATTN_INPUT_SPECS = [
    ("x2T", [D, T], F32),
    ("x2ownT", [512, T], F32),
    ("BwT_in", [1024, 16], F32),
    ("Dcol", [128, 4], F32),
    ("logA_intra", [128, 8192], F32), ("tmat_intra", [128, 8192], F32), ("C_intra", [128, 8192], F32),
    ("logA_rev", [128, 8192], F32), ("tmat_rev", [128, 8192], F32),
    ("logA_ca", [128, 8192], F32), ("tmat_ca", [128, 8192], F32), ("C_ca", [128, 8192], F32),
    ("logA_scan", [128, 64], F32),
    ("cidb", [128, 128], BF16),
    ("ones_bf", [128, 1], BF16), ("ones_row_bf", [1, 128], BF16), ("consts", [128, 8], F32),
]


def host_l1_attn_inputs(core, x2_full, inp):
    b, dh = core // 2, core % 2
    D0 = dh * 512
    logA = np.asarray(inp["L1_attn_log_A_diag"], np.float32)
    Cw = np.asarray(inp["L1_attn_C_w"], np.float32)
    Dv = np.asarray(inp["L1_attn_D_vec"], np.float32)
    Bw = np.asarray(inp["L1_attn_B_w"], np.float32)

    p = np.arange(128)
    s_i = p // 8; dlt = p % 8
    g_i = np.arange(16); d_i = np.arange(512)
    logA_intra = logA[D0 + d_i[None, None, :], s_i[:, None, None]]
    logA_intra = np.broadcast_to(logA_intra, (128, 16, 512)).reshape(128, 8192)
    tmat_intra = np.broadcast_to((8 * g_i[None, :, None] + dlt[:, None, None]).astype(np.float32),
                                 (128, 16, 512)).reshape(128, 8192)
    C_intra = np.broadcast_to(Cw[D0 + d_i[None, None, :], s_i[:, None, None]],
                              (128, 16, 512)).reshape(128, 8192)
    s_r = np.arange(16)
    logA_rev = np.broadcast_to(logA[D0 + d_i[None, None, :], s_r[None, :, None]],
                               (128, 16, 512)).reshape(128, 8192)
    tmat_rev = np.broadcast_to((127 - p)[:, None, None].astype(np.float32),
                               (128, 16, 512)).reshape(128, 8192)
    dpp = p // 16; s_c = p % 16
    g_c = np.arange(64); l_c = np.arange(128)
    logA_ca = np.broadcast_to(logA[D0 + 8 * g_c[None, :, None] + dpp[:, None, None], s_c[:, None, None]],
                              (128, 64, 128)).reshape(128, 8192)
    tmat_ca = np.broadcast_to((l_c + 1)[None, None, :].astype(np.float32),
                              (128, 64, 128)).reshape(128, 8192)
    C_ca = np.broadcast_to(Cw[D0 + 8 * g_c[None, :, None] + dpp[:, None, None], s_c[:, None, None]],
                           (128, 64, 128)).reshape(128, 8192)
    dc_s = np.arange(4); s_s = np.arange(16)
    logA_scan = logA[D0 + dc_s[None, :, None] * 128 + p[:, None, None], s_s[None, None, :]].reshape(128, 64)

    return dict(
        x2T=np.ascontiguousarray(x2_full[b].T),
        x2ownT=np.ascontiguousarray(x2_full[b].T[D0:D0 + 512]),
        BwT_in=np.ascontiguousarray(Bw.T),
        Dcol=np.ascontiguousarray(Dv[D0:D0 + 512].reshape(4, 128).T),
        logA_intra=np.ascontiguousarray(logA_intra), tmat_intra=np.ascontiguousarray(tmat_intra),
        C_intra=np.ascontiguousarray(C_intra),
        logA_rev=np.ascontiguousarray(logA_rev), tmat_rev=np.ascontiguousarray(tmat_rev),
        logA_ca=np.ascontiguousarray(logA_ca), tmat_ca=np.ascontiguousarray(tmat_ca),
        C_ca=np.ascontiguousarray(C_ca),
        logA_scan=np.ascontiguousarray(logA_scan),
        cidb=np.eye(128).astype(BF16NP),
        ones_bf=np.ones((128, 1), BF16NP),
        ones_row_bf=np.ones((1, 128), BF16NP),
        consts=np.tile(np.array([[1e-5, 1.0, 0, 0, 0, 0, 0, 0]], np.float32), (128, 1)),
    )


def build_l1_attn_launch(upto=99):
    nc = bass.Bass("TRN2", target_bir_lowering=False, debug=False)
    io = {}
    for name, shape, dt in L1ATTN_INPUT_SPECS:
        io[name] = nc.dram_tensor(name, shape, dt, kind="ExternalInput").ap()
    io["xout3"] = nc.dram_tensor("xout3", [512, T], F32, kind="ExternalOutput").ap()
    io["cdram"] = nc.dram_tensor("cdram", [128 * 4160], F32, kind="Internal").ap()
    with contextlib.ExitStack() as stack:
        block = stack.enter_context(nc.Block())
        ctx = Ctx(nc, block, stack)
        emit_l1_attn(ctx, io, upto=upto)
    return nc



NW = TH // 512


def emit_l1f_attn(ctx, io, tag, src_amid=False):
    nc, block = ctx.nc, ctx.block
    s = lambda nm: ctx.sem(tag + nm)
    sL = s("L"); sT = s("T"); sV = s("V"); sS = s("S"); sO = s("O"); sG = s("G")
    sb = lambda n, sh, dt: ctx.stack.enter_context(nc.sbuf_tensor(tag + n, sh, dt))
    ps_ = lambda n, sh, dt=F32: ctx.stack.enter_context(nc.psum_tensor(tag + n, sh, dt))

    astw = sb("astw", [128, 8 * 513], F32)
    astbf = sb("astbf", [128, 8 * 513], BF16)
    halobf = sb("halobf", [128, 8 * 513], BF16)
    halo = sb("halo", [128, 8 * 513], F32)
    wqkv = sb("wqkv", [128, 8 * 1536], BF16)
    wo = sb("wo", [128, 4 * 1024], BF16)
    kfm = sb("kfm", [128, 4096], BF16)
    vtm = sb("vtm", [128, 8 * 512], BF16)
    qtm = sb("qtm", [128, 512], F32)
    ktmq = sb("ktmq", [128, 512], F32)
    qbf = sb("qbf", [128, 2048], BF16)
    kbf = sb("kbf", [128, 2048], BF16)
    qfm = sb("qfm", [128, 2048], BF16)
    otm = sb("otm", [128, 512], BF16)
    ofm = sb("ofm", [128, 2048], BF16)
    Ptm = sb("Ptm", [128, 1024], BF16)
    PT = sb("PT", [128, 1024], BF16)
    scl = sb("scl", [128, 512], F32)
    rq = sb("rq", [128, 512], F32)
    t1 = sb("t1", [128, 512], F32)
    sq512 = sb("sq512", [128, 512], F32)
    nsum = sb("nsum", [128, 8], F32)
    qnorm = sb("qnorm", [128, 32], F32)
    knorm = sb("knorm", [128, 32], F32)
    kn32 = sb("kn32", [32, 128], F32)
    kn32m = sb("kn32m", [32, 1], F32)
    knrow = sb("knrow", [1, 32], F32)
    kmax2 = sb("kmax2", [1, 16], F32)
    kmcur = sb("kmcur", [1, 8], F32)
    kmaxb = sb("kmaxb", [128, 8], F32)
    negB = sb("negB", [128, 1], F32)
    rowsum = sb("rowsum", [128, 1], F32)
    rinv = sb("rinv", [128, 1], F32)
    tabs = sb("tabs", [128, 4 * 2048], F32)
    qsrow = sb("qsrow", [1, 1024], F32)
    qst = sb("qst", [128, 1024], F32)
    masks = sb("masks", [128, 12 * 1024], BF16)
    idb = sb("idb", [128, 128], BF16)
    idf = sb("idf", [128, 128], F32)
    onesrf = sb("onesrf", [1, 128], F32)
    cst = sb("cst", [128, 8], F32)

    DBG = "dbgP" in io
    DW, DIC, DH = 2, 1, 3
    pS = ps_("pS", [128, 1024])
    pq3 = ps_("pq3", [128, 512])
    pq4 = ps_("pq4", [128, 512])
    pq5 = ps_("pq5", [128, 512])
    pqb = ps_("pqb", [128, 2048], BF16)
    pt16 = pq5[0:32, 0:128]
    po = pq5[:, 320:384]

    @block.sync
    def _(sync):
        for key, dst in (("masks", masks), ("cidb", idb), ("cidf", idf),
                         ("ones_row_f", onesrf), ("consts", cst), ("qsks", qsrow)):
            sync.dma_start(out=dst[:], in_=io[key]).then_inc(sL.h, sL.bump(16))
        if src_amid:
            src = bass.AP(io["amid"].tensor, 1, [[AMW, 128], [128 * AMW, 8], [1, 513]])
        else:
            src = bass.AP(io["haloin"].tensor, 0, [[513, 128], [128 * 513, 8], [1, 513]])
        dst = bass.AP(halo, 0, [[8 * 513, 128], [513, 8], [1, 513]])
        sync.dma_start(out=dst, in_=src).then_inc(sL.h, sL.bump(16))
        sync.wait_ge(sL.h, sL.n)

    @block.gpsimd
    def _(g):
        if src_amid:
            srch = bass.AP(io["amid"].tensor, 1, [[AMW, 128], [128 * AMW, 8], [1, 513]])
        else:
            srch = bass.AP(io["haloin"].tensor, 0, [[513, 128], [128 * 513, 8], [1, 513]])
        dsth = bass.AP(halobf, 0, [[8 * 513, 128], [513, 8], [1, 513]])
        g.dma_start(out=dsth, in_=srch).then_inc(sG.h, sG.bump(16))
        src = bass.AP(io["wqkv_in"].tensor, 0, [[1536, 128], [128 * 1536, 8], [1, 1536]])
        dst = bass.AP(wqkv, 0, [[8 * 1536, 128], [1536, 8], [1, 1536]])
        g.dma_start(out=dst, in_=src).then_inc(sG.h, sG.bump(16))
        src = bass.AP(io["wo_in"].tensor, 0, [[1024, 128], [128 * 1024, 4], [1, 1024]])
        dst = bass.AP(wo, 0, [[4 * 1024, 128], [1024, 4], [1, 1024]])
        g.dma_start(out=dst, in_=src).then_inc(sG.h, sG.bump(16))
        g.wait_ge(sG.h, sG.n)

    @block.tensor
    def _(tt):
        tt.wait_ge(sL.h, sL.n)
        tt.matmul(pq3[:, 0:512], onesrf[:], qsrow[0:1, 0:512], start=True, stop=True)
        m = tt.matmul(pq4[:, 0:512], onesrf[:], qsrow[0:1, 512:1024], start=True, stop=True)
        m.then_inc(sT.h, sT.bump())

    @block.vector
    def _(v):
        v.wait_ge(sT.h, sT.n)
        v.tensor_copy(qst[:, 0:512], pq3[:, 0:512])
        v.tensor_copy(qst[:, 512:1024], pq4[:, 0:512])
        v.memset(kmax2[:], 0.0)
        v.nop().then_inc(sV.h, sV.bump())

    ctx.barrier()

    for w in range(NW + 1):
        slot = w % 2
        var = 2 if w == 1 else slot

        @block.gpsimd
        def _(g, w=w):
            for tb in range(4):
                src = bass.AP(io["rtabs"].tensor, tb * 4608 * 512 + w * 512 * 512,
                              [[512, 128], [128 * 512, 4], [1, 512]])
                dst = bass.AP(tabs, tb * 2048, [[4 * 2048, 128], [512, 4], [1, 512]])
                g.dma_start(out=dst, in_=src).then_inc(sG.h, sG.bump(16))
            g.wait_ge(sG.h, sG.n)

        if w > 0:
            @block.sync
            def _(sync, w=w):
                sync.wait_ge(sO.h, sO.n)
                if src_amid:
                    src = bass.AP(io["amid"].tensor, 513 + (w - 1) * 512,
                                  [[AMW, 128], [128 * AMW, 8], [1, 513]])
                else:
                    src = bass.AP(io["ain"].tensor, (w - 1) * 512, [[TH + 1, 128], [128 * (TH + 1), 8], [1, 513]])
                dst = bass.AP(astw, 0, [[8 * 513, 128], [513, 8], [1, 513]])
                sync.dma_start(out=dst, in_=src).then_inc(sL.h, sL.bump(16))
                sync.wait_ge(sL.h, sL.n)

            @block.gpsimd
            def _(g, w=w):
                g.wait_ge(sO.h, sO.n)
                if src_amid:
                    src = bass.AP(io["amid"].tensor, 513 + (w - 1) * 512,
                                  [[AMW, 128], [128 * AMW, 8], [1, 513]])
                else:
                    src = bass.AP(io["ain"].tensor, (w - 1) * 512, [[TH + 1, 128], [128 * (TH + 1), 8], [1, 513]])
                dst = bass.AP(astbf, 0, [[8 * 513, 128], [513, 8], [1, 513]])
                g.dma_start(out=dst, in_=src).then_inc(sG.h, sG.bump(16))
                g.wait_ge(sG.h, sG.n)

            ctx.barrier()

        for tc in range(4):
            @block.tensor
            def _(tt, w=w, tc=tc):
                tt.wait_ge(sV.h, sV.n)
                for kk in range(8):
                    base = kk * 513 + (1 if kk < 4 else 0) + tc * 128
                    lhs = (halobf if w == 0 else astbf)[:, base: base + 128]
                    for j, ps in enumerate((pq3, pq4, pq5)):
                        m = tt.matmul(ps[:, 0:512], lhs,
                                      wqkv[:, kk * 1536 + j * 512: kk * 1536 + j * 512 + 512],
                                      start=(kk == 0), stop=(kk == 7))
                m.then_inc(sT.h, sT.bump())

            @block.vector
            def _(v, w=w, tc=tc, slot=slot):
                v.wait_ge(sT.h, sT.n)
                v.tensor_copy(vtm[:, (slot * 4 + tc) * 512:(slot * 4 + tc) * 512 + 512], pq5[:, 0:512])
                if w > 0:
                    v.tensor_copy(qtm[:], pq3[:, 0:512])
                m = v.tensor_copy(ktmq[:], pq4[:, 0:512])
                m.then_inc(sV.h, sV.bump())

            for qk in ((1,) if w == 0 else (0, 1)):
                srcw = qtm if qk == 0 else ktmq
                dstb = qbf if qk == 0 else kbf
                normd = qnorm if qk == 0 else knorm

                @block.scalar
                def _(sc, w=w, tc=tc, qk=qk, srcw=srcw):
                    sc.wait_ge(sV.h, sV.n)
                    sc.wait_ge(sG.h, sG.n)
                    sc.activation(sq512[:], srcw[:], AF.Square).then_inc(sS.h, sS.bump())

                @block.vector
                def _(v, w=w, tc=tc, qk=qk):
                    v.wait_ge(sS.h, sS.n)
                    for h in range(8):
                        r = v.reduce_sum(nsum[:, h:h + 1], sq512[:, h * 64:(h + 1) * 64],
                                         axis=mybir.AxisListType.X)
                    r.then_inc(sV.h, sV.bump())
                    v.wait_ge(sV.h, sV.n)

                @block.scalar
                def _(sc, w=w, tc=tc, qk=qk):
                    sc.activation(nsum[:], nsum[:], AF.Ln, bias=cst[:, 2:3]).then_inc(sS.h, sS.bump())
                    sc.wait_ge(sS.h, sS.n)
                    m = sc.activation(nsum[:], nsum[:], AF.Exp, scale=-0.5)
                    m.then_inc(sS.h, sS.bump())

                @block.vector
                def _(v, w=w, tc=tc, qk=qk, srcw=srcw, dstb=dstb):
                    v.wait_ge(sS.h, sS.n)
                    for h in range(8):
                        v.scalar_tensor_tensor(t1[:, h * 64:(h + 1) * 64], srcw[:, h * 64:(h + 1) * 64],
                                               nsum[:, h:h + 1], qst[:, qk * 512 + h * 64: qk * 512 + (h + 1) * 64],
                                               ALU.mult, ALU.mult)
                    dneg = bass.AP(rq, 0, [[512, 128], [64, 8], [1, 32]])
                    sneg = bass.AP(t1, 32, [[512, 128], [64, 8], [1, 32]])
                    v.tensor_scalar(dneg, sneg, -1.0, None, ALU.mult)
                    dcp = bass.AP(rq, 32, [[512, 128], [64, 8], [1, 32]])
                    scp = bass.AP(t1, 0, [[512, 128], [64, 8], [1, 32]])
                    v.tensor_copy(dcp, scp)
                    tb = 0 if qk == 0 else 2
                    v.tensor_tensor(t1[:], t1[:], tabs[:, tb * 2048 + tc * 512: tb * 2048 + tc * 512 + 512], ALU.mult)
                    v.tensor_tensor(rq[:], rq[:], tabs[:, (tb + 1) * 2048 + tc * 512: (tb + 1) * 2048 + tc * 512 + 512], ALU.mult)
                    v.tensor_tensor(t1[:], t1[:], rq[:], ALU.add)
                    m = v.tensor_copy(dstb[:, tc * 512:(tc + 1) * 512], t1[:])
                    m.then_inc(sV.h, sV.bump())

            @block.tensor
            def _(tt, w=w, tc=tc, slot=slot):
                tt.wait_ge(sV.h, sV.n)
                for p in range(4):
                    m = tt.transpose(pqb[:, p * 128:(p + 1) * 128],
                                     kbf[:, tc * 512 + p * 128: tc * 512 + (p + 1) * 128], idb[:])
                if w > 0:
                    for p in range(4):
                        m = tt.transpose(pqb[:, 1024 + p * 128: 1024 + (p + 1) * 128],
                                         qbf[:, tc * 512 + p * 128: tc * 512 + (p + 1) * 128], idb[:])
                m.then_inc(sT.h, sT.bump())

            @block.vector
            def _(v, w=w, tc=tc, slot=slot):
                v.wait_ge(sT.h, sT.n)
                dstk = bass.AP(kfm, slot * 512 + tc * 128, [[4096, 128], [1024, 4], [1, 128]])
                srck = bass.AP(pqb, 0, [[2048, 128], [128, 4], [1, 128]])
                m = v.tensor_copy(dstk, srck)
                if w > 0:
                    dstq = bass.AP(qfm, tc * 128, [[2048, 128], [512, 4], [1, 128]])
                    srcq = bass.AP(pqb, 1024, [[2048, 128], [128, 4], [1, 128]])
                    m = v.tensor_copy(dstq, srcq)
                m.then_inc(sV.h, sV.bump())

        if w > 0:
            if DBG and w == DW:
                @block.sync
                def _(sync):
                    sync.wait_ge(sV.h, sV.n)
                    sync.dma_start(out=io["dbgKV"][:, 0:4096], in_=kfm[:]).then_inc(sO.h, sO.bump(16))
                    sync.dma_start(out=io["dbgKV"][:, 4096:8192], in_=vtm[:]).then_inc(sO.h, sO.bump(16))
                    sync.dma_start(out=io["dbgKV"][:, 8192:10240], in_=qfm[:]).then_inc(sO.h, sO.bump(16))
                    sync.wait_ge(sO.h, sO.n)
            for ic in range(4):
                for h in range(8):
                    rt, poff = h // 2, (h % 2) * 64

                    @block.tensor
                    def _(tt, w=w, ic=ic, h=h, rt=rt, poff=poff, var=var):
                        tt.wait_ge(sS.h, sS.n)
                        tt.wait_ge(sV.h, sV.n)
                        if DBG:
                            tt.wait_ge(sO.h, sO.n)
                        for sl2 in range(2):
                            tt.matmul(pS[:, sl2 * 512:(sl2 + 1) * 512],
                                      qfm[poff:poff + 64, rt * 512 + ic * 128: rt * 512 + ic * 128 + 128],
                                      kfm[poff:poff + 64, rt * 1024 + sl2 * 512: rt * 1024 + (sl2 + 1) * 512],
                                      start=True, stop=True)
                            m = tt.matmul(pS[:, sl2 * 512:(sl2 + 1) * 512], idb[:],
                                          masks[:, (var * 4 + ic) * 1024 + sl2 * 512:
                                                (var * 4 + ic) * 1024 + (sl2 + 1) * 512],
                                          start=False, stop=True)
                        m.then_inc(sT.h, sT.bump())

                    @block.vector
                    def _(v, w=w, ic=ic, h=h):
                        v.wait_ge(sT.h, sT.n)
                        r = v.reduce_max(negB[:], pS[:, 0:1024], axis=mybir.AxisListType.X)
                        r.then_inc(sV.h, sV.bump())
                        v.wait_ge(sV.h, sV.n)
                        r2 = v.tensor_scalar(negB[:], negB[:], -1.0, None, ALU.mult)
                        r2.then_inc(sV.h, sV.bump())

                    @block.scalar
                    def _(sc, w=w, ic=ic, h=h):
                        sc.wait_ge(sV.h, sV.n)
                        if DBG:
                            sc.wait_ge(sO.h, sO.n)
                        m = sc.activation(Ptm[:], pS[:, 0:1024], AF.Exp, bias=negB[:],
                                          accum_out=rowsum[:])
                        m.then_inc(sS.h, sS.bump())

                    if DBG and w == DW and ic == DIC and h == DH:
                        @block.vector
                        def _(v):
                            v.wait_ge(sS.h, sS.n)
                            v.tensor_copy(t1[:], pS[:, 0:512])
                            m = v.tensor_copy(sq512[:], pS[:, 512:1024])
                            m.then_inc(sV.h, sV.bump())

                        @block.sync
                        def _(sync):
                            sync.wait_ge(sS.h, sS.n)
                            sync.wait_ge(sV.h, sV.n)
                            sync.dma_start(out=io["dbgP"][:, 0:512], in_=t1[:]).then_inc(sO.h, sO.bump(16))
                            sync.dma_start(out=io["dbgP"][:, 512:1024], in_=sq512[:]).then_inc(sO.h, sO.bump(16))
                            with nc.allow_non_contiguous_dma(reason="dbg"):
                                sync.dma_start(out=io["dbgP"][:, 1024:1025], in_=negB[:]).then_inc(sO.h, sO.bump(16))
                                sync.dma_start(out=io["dbgP"][:, 1025:1026], in_=rowsum[:]).then_inc(sO.h, sO.bump(16))
                            sync.dma_start(out=io["dbgPt"][:, 0:1024], in_=Ptm[:]).then_inc(sO.h, sO.bump(16))
                            sync.wait_ge(sO.h, sO.n)

                    @block.vector
                    def _(v, w=w, ic=ic, h=h):
                        v.wait_ge(sS.h, sS.n)
                        m = v.reciprocal(rinv[:], rowsum[:])
                        m.then_inc(sV.h, sV.bump())

                    @block.tensor
                    def _(tt, w=w, ic=ic, h=h):
                        tt.wait_ge(sS.h, sS.n)
                        for t_ in range(4):
                            m = tt.transpose(pqb[:, t_ * 128:(t_ + 1) * 128],
                                             Ptm[:, t_ * 128:(t_ + 1) * 128], idb[:])
                        m.then_inc(sT.h, sT.bump())

                    @block.vector
                    def _(v, w=w, ic=ic, h=h):
                        v.wait_ge(sT.h, sT.n)
                        m = v.tensor_copy(PT[:, 0:512], pqb[:, 0:512])
                        m.then_inc(sV.h, sV.bump())

                    @block.tensor
                    def _(tt, w=w, ic=ic, h=h):
                        tt.wait_ge(sV.h, sV.n)
                        for t_ in range(4):
                            m = tt.transpose(pqb[:, 1024 + t_ * 128: 1024 + (t_ + 1) * 128],
                                             Ptm[:, 512 + t_ * 128: 512 + (t_ + 1) * 128], idb[:])
                        m.then_inc(sT.h, sT.bump())

                    @block.vector
                    def _(v, w=w, ic=ic, h=h):
                        v.wait_ge(sT.h, sT.n)
                        m = v.tensor_copy(PT[:, 512:1024], pqb[:, 1024:1536])
                        m.then_inc(sV.h, sV.bump())

                    @block.tensor
                    def _(tt, w=w, ic=ic, h=h):
                        tt.wait_ge(sV.h, sV.n)
                        for jc in range(8):
                            m = tt.matmul(po, PT[:, jc * 128:(jc + 1) * 128],
                                          vtm[:, jc * 512 + h * 64: jc * 512 + (h + 1) * 64],
                                          start=(jc == 0), stop=(jc == 7))
                        m.then_inc(sT.h, sT.bump())

                    @block.vector
                    def _(v, w=w, ic=ic, h=h):
                        v.wait_ge(sT.h, sT.n)
                        if DBG:
                            v.wait_ge(sO.h, sO.n)
                        m = v.tensor_scalar(otm[:, h * 64:(h + 1) * 64], po, rinv[:], None, ALU.mult)
                        m.then_inc(sV.h, sV.bump())

                if DBG and w == DW and ic == DIC:
                    @block.sync
                    def _(sync):
                        sync.wait_ge(sV.h, sV.n)
                        sync.dma_start(out=io["dbgPt"][:, 1024:1536], in_=otm[:]).then_inc(sO.h, sO.bump(16))
                        sync.wait_ge(sO.h, sO.n)

                @block.tensor
                def _(tt, w=w, ic=ic):
                    tt.wait_ge(sV.h, sV.n)
                    for p in range(4):
                        m = tt.transpose(pqb[:, p * 128:(p + 1) * 128],
                                         otm[:, p * 128:(p + 1) * 128], idb[:])
                    m.then_inc(sT.h, sT.bump())

                @block.vector
                def _(v, w=w, ic=ic):
                    v.wait_ge(sT.h, sT.n)
                    dsto = bass.AP(ofm, ic * 128, [[2048, 128], [512, 4], [1, 128]])
                    srco = bass.AP(pqb, 0, [[2048, 128], [128, 4], [1, 128]])
                    m = v.tensor_copy(dsto, srco)
                    m.then_inc(sV.h, sV.bump())

            for dto in range(8):
                @block.tensor
                def _(tt, w=w, dto=dto):
                    if dto == 0:
                        tt.wait_ge(sV.h, sV.n)
                    elif dto >= 2:
                        tt.wait_ge(sV.h, sV.n - 1)
                    ps = pq3 if dto % 2 == 0 else pq4
                    for kk in range(4):
                        m = tt.matmul(ps[:, 0:512], wo[:, kk * 1024 + dto * 128: kk * 1024 + dto * 128 + 128],
                                      ofm[:, kk * 512:(kk + 1) * 512], start=(kk == 0), stop=(kk == 3))
                    m.then_inc(sT.h, sT.bump())

                @block.vector
                def _(v, w=w, dto=dto):
                    v.wait_ge(sT.h, sT.n)
                    ps = pq3 if dto % 2 == 0 else pq4
                    sl = slice(dto * 513 + 1, dto * 513 + 513)
                    m = v.tensor_tensor(astw[:, sl], astw[:, sl], ps[:, 0:512], ALU.add)
                    m.then_inc(sV.h, sV.bump())

            @block.sync
            def _(sync, w=w):
                sync.wait_ge(sV.h, sV.n)
                dst = bass.AP(io["aout"].tensor, (w - 1) * 512, [[TH, 128], [128 * TH, 8], [1, 512]])
                srca = bass.AP(astw, 1, [[8 * 513, 128], [513, 8], [1, 512]])
                sync.dma_start(out=dst, in_=srca).then_inc(sO.h, sO.bump(16))

        ctx.barrier()

    @block.sync
    def _(sync):
        sync.wait_ge(sO.h, sO.n)

    ctx.barrier()



def emit_l1f_mlp(ctx, io, tag, res_add=False):
    nc, block = ctx.nc, ctx.block
    s = lambda nm: ctx.sem(tag + nm)
    sL = s("L"); sT = s("T"); sV = s("V"); sS = s("S"); sO = s("O"); sG = s("G")
    sb = lambda n, sh, dt: ctx.stack.enter_context(nc.sbuf_tensor(tag + n, sh, dt))
    ps_ = lambda n, sh, dt=F32: ctx.stack.enter_context(nc.psum_tensor(tag + n, sh, dt))
    FP = 5632
    NT = TH // 512
    rbuf = sb("rbuf", [128, 8 * 512], F32) if res_add else None

    w1s = sb("w1", [128, 8 * FP], BF16)
    w2a = sb("w2a", [128, 1024], BF16)
    w2b = sb("w2b", [128, 1024], BF16)
    xbf = sb("xbf", [128, 8 * 512], BF16)
    xf = sb("xf", [128, 8 * 512], F32)
    sq = sb("sq", [128, 8 * 512], BF16)
    hs = sb("hs", [128, 22 * 512], BF16)
    rows = sb("rows", [1, 2 * 512], F32)
    rbf = sb("rbf", [1, 1024], BF16)
    gb = sb("gb", [128, 16], F32)
    b1c = sb("b1c", [128, 44], F32)
    b2c = sb("b2c", [128, 8], F32)
    tmp512 = sb("tmp", [128, 512], F32)
    gel = sb("gel", [64, 512], F32)
    onesc = sb("onesc", [128, 1], BF16)
    onesr = sb("onesr", [1, 128], BF16)
    cst = sb("cst", [128, 8], F32)
    pA = ps_("pA", [128, 2048])
    pB = ps_("pB", [128, 2048])
    MU, TMP = 0, 512

    @block.sync
    def _(sync):
        for key, dst in (("gamma_col", gb), ("b1_col", b1c), ("b2_col", b2c),
                         ("ones_bf", onesc), ("ones_row_bf", onesr), ("consts", cst)):
            sync.dma_start(out=dst[:] if key != "gamma_col" else gb[:, 0:8], in_=io[key]).then_inc(sL.h, sL.bump(16))
        sync.dma_start(out=gb[:, 8:16], in_=io["beta_col"]).then_inc(sL.h, sL.bump(16))
        sync.wait_ge(sL.h, sL.n)

    @block.gpsimd
    def _(g):
        src = bass.AP(io["w1_in"].tensor, 0, [[FP, 128], [128 * FP, 8], [1, FP]])
        dst = bass.AP(w1s, 0, [[8 * FP, 128], [FP, 8], [1, FP]])
        g.dma_start(out=dst, in_=src).then_inc(sG.h, sG.bump(16))
        g.wait_ge(sG.h, sG.n)

    for kf in range(22):
        buf0 = w2a if kf % 2 == 0 else w2b

        @block.gpsimd
        def _(g, kf=kf, buf0=buf0):
            if kf >= 2:
                g.wait_ge(sL.h, sL.n - 16)
            g.dma_start(out=buf0[:], in_=io["w2_in"][kf * 128:(kf + 1) * 128, :]).then_inc(
                sG.h, sG.bump(16))

        @block.sync
        def _(sync, kf=kf, buf0=buf0):
            sync.wait_ge(sG.h, sG.n)
            sync.dma_start(out=io["w2s"][kf * 128:(kf + 1) * 128, :], in_=buf0[:]).then_inc(
                sL.h, sL.bump(16))

    @block.sync
    def _(sync):
        sync.wait_ge(sL.h, sL.n)

    ctx.barrier()

    for t in range(NT):
        tok0 = t * 512

        @block.sync
        def _(sync, tok0=tok0):
            src = bass.AP(io["min_"].tensor, 1 + tok0, [[TH + 1, 128], [128 * (TH + 1), 8], [1, 512]])
            dst = bass.AP(xf, 0, [[8 * 512, 128], [512, 8], [1, 512]])
            sync.dma_start(out=dst, in_=src).then_inc(sL.h, sL.bump(16))
            if res_add:
                src2 = bass.AP(io["resid"].tensor, tok0, [[TH, 128], [128 * TH, 8], [1, 512]])
                dst2 = bass.AP(rbuf, 0, [[8 * 512, 128], [512, 8], [1, 512]])
                sync.dma_start(out=dst2, in_=src2).then_inc(sL.h, sL.bump(16))
            sync.wait_ge(sL.h, sL.n)

        @block.gpsimd
        def _(g, tok0=tok0):
            src1 = bass.AP(io["min_"].tensor, 1 + tok0, [[TH + 1, 128], [128 * (TH + 1), 4], [1, 512]])
            dst1 = bass.AP(xbf, 0, [[8 * 512, 128], [512, 4], [1, 512]])
            g.dma_start(out=dst1, in_=src1).then_inc(sG.h, sG.bump(16))
            src2 = bass.AP(io["min_"].tensor, 4 * 128 * (TH + 1) + tok0, [[TH + 1, 128], [128 * (TH + 1), 4], [1, 512]])
            dst2 = bass.AP(xbf, 4 * 512, [[8 * 512, 128], [512, 4], [1, 512]])
            g.dma_start(out=dst2, in_=src2).then_inc(sG.h, sG.bump(16))
            g.wait_ge(sG.h, sG.n)

        ctx.barrier()

        @block.scalar
        def _(sc, t=t):
            sc.activation(sq[:], xbf[:], AF.Square).then_inc(sS.h, sS.bump())

        @block.tensor
        def _(tt, t=t):
            for dt in range(8):
                tt.matmul(pA[0:1, 0:512], onesc[:], xbf[:, dt * 512:(dt + 1) * 512],
                          start=(dt == 0), stop=(dt == 7))
            tt.wait_ge(sS.h, sS.n)
            for dt in range(8):
                m = tt.matmul(pA[0:1, 512:1024], onesc[:], sq[:, dt * 512:(dt + 1) * 512],
                              start=(dt == 0), stop=(dt == 7))
            m.then_inc(sT.h, sT.bump())

        @block.vector
        def _(v, t=t):
            v.wait_ge(sT.h, sT.n)
            v.tensor_scalar(rows[:, MU:MU + 512], pA[0:1, 0:512], 1.0 / D, None, ALU.mult)
            v.tensor_scalar(rows[:, TMP:TMP + 512], pA[0:1, 512:1024], 1.0 / D, None, ALU.mult)
            v.tensor_tensor(tmp512[0:1, 0:512], rows[:, MU:MU + 512], rows[:, MU:MU + 512], ALU.mult)
            v.tensor_tensor(rows[:, TMP:TMP + 512], rows[:, TMP:TMP + 512], tmp512[0:1, 0:512],
                            ALU.subtract).then_inc(sV.h, sV.bump())

        @block.scalar
        def _(sc, t=t):
            sc.wait_ge(sV.h, sV.n)
            sc.activation(rows[:, TMP:TMP + 512], rows[:, TMP:TMP + 512], AF.Ln, bias=cst[0:1, 0:1]).then_inc(sS.h, sS.bump())
            sc.wait_ge(sS.h, sS.n)
            sc.activation(rows[:, TMP:TMP + 512], rows[:, TMP:TMP + 512], AF.Exp,
                          scale=-0.5).then_inc(sS.h, sS.bump())

        @block.vector
        def _(v, t=t):
            v.wait_ge(sS.h, sS.n)
            v.tensor_copy(rbf[:, 0:512], rows[:, TMP:TMP + 512])
            v.tensor_tensor(rbf[:, 512:1024], rows[:, MU:MU + 512], rows[:, TMP:TMP + 512],
                            ALU.mult).then_inc(sV.h, sV.bump())

        @block.tensor
        def _(tt, t=t):
            tt.wait_ge(sV.h, sV.n)
            tt.matmul(pB[:, 0:512], onesr[:], rbf[:, 0:512], start=True, stop=True)
            m = tt.matmul(pB[:, 512:1024], onesr[:], rbf[:, 512:1024], start=True, stop=True)
            m.then_inc(sT.h, sT.bump())

        @block.vector
        def _(v, t=t):
            v.wait_ge(sT.h, sT.n)
            for dt in range(8):
                sl = slice(dt * 512, (dt + 1) * 512)
                v.tensor_tensor(xbf[:, sl], xbf[:, sl], pB[:, 0:512], ALU.mult)
                v.tensor_tensor(xbf[:, sl], xbf[:, sl], pB[:, 512:1024], ALU.subtract)
                v.tensor_scalar(xbf[:, sl], xbf[:, sl], gb[:, dt:dt + 1], gb[:, 8 + dt:8 + dt + 1],
                                ALU.mult, ALU.add)
            v.nop().then_inc(sV.h, sV.bump())

        ctx.barrier()

        for m_ in range(44):
            @block.tensor
            def _(tt, m_=m_, t=t):
                if m_ >= 2:
                    tt.wait_ge(sV.h, sV.n - 1)
                ps = pA[:, (m_ % 2) * 512:(m_ % 2) * 512 + 512]
                for g in range(8):
                    mm = tt.matmul(ps, w1s[:, g * FP + m_ * 128: g * FP + (m_ + 1) * 128],
                                   xbf[:, g * 512:(g + 1) * 512], start=(g == 0), stop=(g == 7))
                mm.then_inc(sT.h, sT.bump())

            @block.scalar
            def _(sc, m_=m_, t=t):
                sc.wait_ge(sT.h, sT.n)
                ps = pA[:, (m_ % 2) * 512:(m_ % 2) * 512 + 512]
                mm = sc.activation(gel[:], pA[64:128, (m_ % 2) * 512:(m_ % 2) * 512 + 512],
                                   AF.Gelu, bias=b1c[64:128, m_:m_ + 1])
                mm.then_inc(sS.h, sS.bump())

            @block.vector
            def _(v, m_=m_, t=t):
                v.wait_ge(sS.h, sS.n)
                ps = pA[:, (m_ % 2) * 512:(m_ % 2) * 512 + 512]
                dst = hs[(m_ % 2) * 64:(m_ % 2) * 64 + 64, (m_ // 2) * 512:(m_ // 2 + 1) * 512]
                v.scalar_tensor_tensor(tmp512[0:64, :], ps[0:64, 0:512], b1c[0:64, m_:m_ + 1],
                                       gel[:], ALU.add, ALU.mult)
                mm = v.tensor_copy(dst, tmp512[0:64, :])
                mm.then_inc(sV.h, sV.bump())

        ctx.barrier()

        for p_ in range(2):
            for kf in range(22):
                @block.sync
                def _(sync, p_=p_, kf=kf, t=t):
                    buf = w2a if kf % 2 == 0 else w2b
                    sync.wait_ge(sT.h, sT.n - 1)
                    sync.dma_start(out=buf[:], in_=io["w2s"][kf * 128:(kf + 1) * 128, :]).then_inc(
                        sL.h, sL.bump(16))

                @block.tensor
                def _(tt, p_=p_, kf=kf, t=t):
                    buf = w2a if kf % 2 == 0 else w2b
                    tt.wait_ge(sL.h, sL.n)
                    if kf == 0:
                        tt.wait_ge(sV.h, sV.n)
                    for dq in range(4):
                        do = p_ * 4 + dq
                        mm = tt.matmul(pB[:, dq * 512:(dq + 1) * 512],
                                       buf[:, do * 128:(do + 1) * 128],
                                       hs[:, kf_cols(kf)], start=(kf == 0), stop=(kf == 21))
                    mm.then_inc(sT.h, sT.bump())

            for dq in range(4):
                @block.vector
                def _(v, p_=p_, dq=dq, t=t):
                    v.wait_ge(sT.h, sT.n)
                    if res_add:
                        v.wait_ge(sL.h, sL.n)
                    do = p_ * 4 + dq
                    ps = pB[:, dq * 512:(dq + 1) * 512]
                    sl = slice(do * 512, (do + 1) * 512)
                    v.tensor_scalar(tmp512[:], ps, 1.0, b2c[:, do:do + 1], ALU.mult, ALU.add)
                    mm = v.tensor_tensor(xf[:, sl], xf[:, sl], tmp512[:], ALU.add)
                    if res_add:
                        mm = v.tensor_tensor(xf[:, sl], xf[:, sl], rbuf[:, sl], ALU.add)
                    mm.then_inc(sV.h, sV.bump())

        @block.sync
        def _(sync, t=t, tok0=tok0):
            sync.wait_ge(sV.h, sV.n)
            dst = bass.AP(io["mout"].tensor, tok0, [[TH, 128], [128 * TH, 8], [1, 512]])
            src = bass.AP(xf, 0, [[8 * 512, 128], [512, 8], [1, 512]])
            sync.dma_start(out=dst, in_=src).then_inc(sO.h, sO.bump(16))
            sync.wait_ge(sO.h, sO.n)

        ctx.barrier()


def kf_cols(kf):
    return slice(kf * 512, (kf + 1) * 512)



def emit_ln(ctx, io, tag):
    nc, block = ctx.nc, ctx.block
    s = lambda nm: ctx.sem(tag + nm)
    sL = s("L"); sS = s("S"); sV = s("V"); sO = s("O")
    sb = lambda n, sh, dt: ctx.stack.enter_context(nc.sbuf_tensor(tag + n, sh, dt))

    xt = sb("xt", [128, 1024], F32)
    ot = sb("ot", [128, 1024], F32)
    sq = sb("sq", [128, 1024], F32)
    s1 = sb("s1", [128, 1], F32)
    s2 = sb("s2", [128, 1], F32)
    tmp = sb("tmp", [128, 1], F32)
    cst = sb("cst", [128, 8], F32)

    @block.sync
    def _(sync):
        sync.dma_start(out=cst[:], in_=io["consts"]).then_inc(sL.h, sL.bump(16))
        sync.wait_ge(sL.h, sL.n)

    ctx.barrier()

    NTILE = TH // 128
    for t in range(NTILE):
        @block.sync
        def _(sync, t=t):
            sync.wait_ge(sO.h, sO.n)
            src = bass.AP(io["xin"].tensor, t * 128 * 1024, [[1024, 128], [1, 1024]])
            sync.dma_start(out=xt[:], in_=src).then_inc(sL.h, sL.bump(16))

        @block.scalar
        def _(sc, t=t):
            sc.wait_ge(sL.h, sL.n)
            sc.wait_ge(sV.h, sV.n)
            sc.activation(sq[:], xt[:], AF.Square).then_inc(sS.h, sS.bump())

        @block.vector
        def _(v, t=t):
            v.wait_ge(sL.h, sL.n)
            r = v.reduce_sum(s1[:], xt[:], axis=mybir.AxisListType.X)
            r.then_inc(sV.h, sV.bump())
            v.wait_ge(sV.h, sV.n)
            v.wait_ge(sS.h, sS.n)
            r = v.reduce_sum(s2[:], sq[:], axis=mybir.AxisListType.X)
            r.then_inc(sV.h, sV.bump())
            v.wait_ge(sV.h, sV.n)
            r = v.tensor_scalar(s1[:], s1[:], -1.0 / 1024.0, None, ALU.mult)
            r.then_inc(sV.h, sV.bump())
            v.wait_ge(sV.h, sV.n)
            r = v.tensor_scalar(s2[:], s2[:], 1.0 / 1024.0, None, ALU.mult)
            r.then_inc(sV.h, sV.bump())
            v.wait_ge(sV.h, sV.n)
            r = v.tensor_tensor(tmp[:], s1[:], s1[:], ALU.mult)
            r.then_inc(sV.h, sV.bump())
            v.wait_ge(sV.h, sV.n)
            r = v.tensor_tensor(s2[:], s2[:], tmp[:], ALU.subtract)
            r.then_inc(sV.h, sV.bump())

        @block.scalar
        def _(sc, t=t):
            sc.wait_ge(sV.h, sV.n)
            r = sc.activation(s2[:], s2[:], AF.Ln, bias=cst[:, 0:1])
            r.then_inc(sS.h, sS.bump())
            sc.wait_ge(sS.h, sS.n)
            r = sc.activation(s2[:], s2[:], AF.Exp, scale=-0.5)
            r.then_inc(sS.h, sS.bump())

        @block.vector
        def _(v, t=t):
            v.wait_ge(sS.h, sS.n)
            v.tensor_scalar(ot[:], xt[:], 1.0, s1[:], ALU.mult, ALU.add)
            r = v.tensor_scalar(ot[:], ot[:], 1.0, s2[:], ALU.mult, ALU.mult)
            r.then_inc(sV.h, sV.bump())

        @block.sync
        def _(sync, t=t):
            sync.wait_ge(sV.h, sV.n)
            dst = bass.AP(io["lnout"].tensor, t * 128 * 1024, [[1024, 128], [1, 1024]])
            sync.dma_start(out=dst, in_=ot[:]).then_inc(sO.h, sO.bump(16))

    @block.sync
    def _(sync):
        sync.wait_ge(sO.h, sO.n)

    ctx.barrier()


AMW = 4737
XEROWS = 4736


def emit_ln_fm(ctx, io, tag):
    nc, block = ctx.nc, ctx.block
    s = lambda nm: ctx.sem(tag + nm)
    sL = s("L"); sS = s("S"); sV = s("V"); sT = s("T"); sO = s("O")
    sb = lambda n, sh, dt: ctx.stack.enter_context(nc.sbuf_tensor(tag + n, sh, dt))

    xt = sb("xt", [128, 1024], F32)
    ot = sb("ot", [128, 1024], F32)
    otf = sb("otf", [128, 1024], F32)
    sq = sb("sq", [128, 1024], F32)
    s1 = sb("s1", [128, 1], F32)
    s2 = sb("s2", [128, 1], F32)
    tmp = sb("tmp", [128, 1], F32)
    cst = sb("cst", [128, 8], F32)
    idfl = sb("idfl", [128, 128], F32)
    zc = sb("zc", [128, 1], F32)
    pT = ctx.stack.enter_context(nc.psum_tensor(tag + "pT", [128, 512], F32))

    @block.sync
    def _(sync):
        sync.dma_start(out=cst[:], in_=io["consts"]).then_inc(sL.h, sL.bump(16))
        sync.dma_start(out=idfl[:], in_=io["cidf"]).then_inc(sL.h, sL.bump(16))
        sync.wait_ge(sL.h, sL.n)

    @block.vector
    def _(v):
        v.wait_ge(sL.h, sL.n)
        m = v.memset(zc[:], 0.0)
        m.then_inc(sV.h, sV.bump())

    @block.sync
    def _(sync):
        sync.wait_ge(sV.h, sV.n)
        with nc.allow_non_contiguous_dma(reason="amid col0 zero"):
            for fb in range(8):
                sync.dma_start(out=bass.AP(io["amid"].tensor, fb * 128 * AMW,
                                           [[AMW, 128], [1, 1]]),
                               in_=zc[:]).then_inc(sO.h, sO.bump(16))
        sync.wait_ge(sO.h, sO.n)

    ctx.barrier()

    for t in range(XEROWS // 128):
        @block.sync
        def _(sync, t=t):
            sync.wait_ge(sO.h, sO.n)
            src = bass.AP(io["xin_ext"].tensor, t * 128 * 1024, [[1024, 128], [1, 1024]])
            sync.dma_start(out=xt[:], in_=src).then_inc(sL.h, sL.bump(16))

        @block.scalar
        def _(sc, t=t):
            sc.wait_ge(sL.h, sL.n)
            sc.wait_ge(sV.h, sV.n)
            sc.activation(sq[:], xt[:], AF.Square).then_inc(sS.h, sS.bump())

        @block.vector
        def _(v, t=t):
            v.wait_ge(sL.h, sL.n)
            r = v.reduce_sum(s1[:], xt[:], axis=mybir.AxisListType.X)
            r.then_inc(sV.h, sV.bump())
            v.wait_ge(sV.h, sV.n)
            v.wait_ge(sS.h, sS.n)
            r = v.reduce_sum(s2[:], sq[:], axis=mybir.AxisListType.X)
            r.then_inc(sV.h, sV.bump())
            v.wait_ge(sV.h, sV.n)
            r = v.tensor_scalar(s1[:], s1[:], -1.0 / 1024.0, None, ALU.mult)
            r.then_inc(sV.h, sV.bump())
            v.wait_ge(sV.h, sV.n)
            r = v.tensor_scalar(s2[:], s2[:], 1.0 / 1024.0, None, ALU.mult)
            r.then_inc(sV.h, sV.bump())
            v.wait_ge(sV.h, sV.n)
            r = v.tensor_tensor(tmp[:], s1[:], s1[:], ALU.mult)
            r.then_inc(sV.h, sV.bump())
            v.wait_ge(sV.h, sV.n)
            r = v.tensor_tensor(s2[:], s2[:], tmp[:], ALU.subtract)
            r.then_inc(sV.h, sV.bump())

        @block.scalar
        def _(sc, t=t):
            sc.wait_ge(sV.h, sV.n)
            r = sc.activation(s2[:], s2[:], AF.Ln, bias=cst[:, 0:1])
            r.then_inc(sS.h, sS.bump())
            sc.wait_ge(sS.h, sS.n)
            r = sc.activation(s2[:], s2[:], AF.Exp, scale=-0.5)
            r.then_inc(sS.h, sS.bump())

        @block.vector
        def _(v, t=t):
            v.wait_ge(sS.h, sS.n)
            v.wait_ge(sT.h, sT.n)
            v.tensor_scalar(ot[:], xt[:], 1.0, s1[:], ALU.mult, ALU.add)
            r = v.tensor_scalar(ot[:], ot[:], 1.0, s2[:], ALU.mult, ALU.mult)
            r.then_inc(sV.h, sV.bump())

        for rnd in range(2):
            @block.tensor
            def _(tt, t=t, rnd=rnd):
                tt.wait_ge(sV.h, sV.n)
                for k in range(4):
                    fb = rnd * 4 + k
                    m = tt.transpose(pT[:, k * 128:(k + 1) * 128],
                                     ot[:, fb * 128:(fb + 1) * 128], idfl[:])
                m.then_inc(sT.h, sT.bump())

            @block.vector
            def _(v, t=t, rnd=rnd):
                v.wait_ge(sT.h, sT.n)
                m = v.tensor_copy(otf[:, rnd * 512:(rnd + 1) * 512], pT[:])
                m.then_inc(sV.h, sV.bump())

        @block.sync
        def _(sync, t=t):
            sync.wait_ge(sV.h, sV.n)
            for fb in range(8):
                dst = bass.AP(io["amid"].tensor, fb * 128 * AMW + 1 + t * 128,
                              [[AMW, 128], [1, 128]])
                sync.dma_start(out=dst, in_=otf[:, fb * 128:(fb + 1) * 128]).then_inc(
                    sO.h, sO.bump(16))

    @block.sync
    def _(sync):
        sync.wait_ge(sO.h, sO.n)

    ctx.barrier()


def build_ln_launch():
    nc = bass.Bass("TRN2", target_bir_lowering=False, debug=False)
    io = {}
    io["xin"] = nc.dram_tensor("xin", [TH, D], F32, kind="ExternalInput").ap()
    io["consts"] = nc.dram_tensor("consts", [128, 8], F32, kind="ExternalInput").ap()
    io["lnout"] = nc.dram_tensor("lnout", [TH, D], F32, kind="ExternalOutput").ap()
    with contextlib.ExitStack() as stack:
        block = stack.enter_context(nc.Block())
        ctx = Ctx(nc, block, stack)
        emit_ln(ctx, io, "ln")
    return nc


def host_ln_inputs(core, x_full):
    b, hh = core // 2, core % 2
    return dict(
        xin=np.ascontiguousarray(x_full[b, hh * TH:(hh + 1) * TH, :]),
        consts=np.tile(np.array([[1e-5, 1.0, 1e-24, 0, 0, 0, 0, 0]], np.float32), (128, 1)),
    )



L1FA_INPUT_SPECS = [
    ("ain", [D, TH + 1], F32),
    ("haloin", [D, 513], F32),
    ("wqkv_in", [1024, 1536], F32),
    ("wo_in", [512, 1024], F32),
    ("rtabs", [4 * 4608, 512], F32),
    ("qsks", [1, 1024], F32),
    ("masks", [12 * 128, 1024], BF16),
    ("cidb", [128, 128], BF16), ("cidf", [128, 128], F32),
    ("ones_row_f", [1, 128], F32), ("consts", [128, 8], F32),
]

WINDOW = 512
DIM_HEAD = 64


def l1f_rotary_tables(half):
    n = T
    half_idx = np.arange(0, DIM_HEAD, 2, dtype=np.float64)
    inv_freq = 1.0 / 10000.0 ** (half_idx / DIM_HEAD)
    t0 = half * TH - 512
    t_ = np.arange(t0, t0 + 4608, dtype=np.float64)
    freqs = t_[:, None] * inv_freq[None, :]
    freqs = np.concatenate([freqs, freqs], axis=-1)
    base = (half_idx + 0.4 * DIM_HEAD) / (1.4 * DIM_HEAD)
    power = (t_ - n // 2) / float(WINDOW // 2)
    scale = base[None, :] ** power[:, None]
    scale = np.concatenate([scale, scale], axis=-1)
    cos, sin = np.cos(freqs), np.sin(freqs)
    QK = 8.0
    cq = (cos * scale * QK).astype(np.float32)
    sq_ = (sin * scale * QK).astype(np.float32)
    ck = (cos / scale).astype(np.float32)
    sk = (sin / scale).astype(np.float32)
    tile8 = lambda a: np.tile(a, (1, 8))
    return np.concatenate([tile8(cq), tile8(sq_), tile8(ck), tile8(sk)], axis=0)


def l1f_masks(half):
    out = np.zeros((12, 128, 1024), np.float32)
    for var in range(3):
        for ic in range(4):
            i_tok = ic * 128 + np.arange(128)[:, None]
            jbuf = np.arange(1024)[None, :]
            jslot = jbuf // 512
            jj = jbuf % 512
            if var == 2:
                curslot = 1
            else:
                curslot = var
            jtok = np.where(jslot == curslot, jj, jj - 512)
            valid = (jtok <= i_tok) & (jtok >= i_tok - WINDOW)
            if var == 2 and half == 0:
                valid = valid & (jslot == curslot)
            out[var * 4 + ic] = np.where(valid, 0.0, -1e9)
    return out.transpose(1, 0, 2).reshape(12 * 128, 1024).astype(BF16NP)


def host_l1f_attn_inputs(core, x_stream, layer, inp, guard_prev, halo_prev):
    b, hh = core // 2, core % 2
    lo = hh * TH
    xT = x_stream[b].T
    ain = np.zeros((D, TH + 1), np.float32)
    ain[:, 1:] = xT[:, lo:lo + TH]
    if hh == 1:
        ain[:, 0] = xT[:, lo - 1]
    halo = np.zeros((D, 513), np.float32)
    if hh == 1:
        halo[:, :] = xT[:, lo - 514: lo - 1]
        halo[:, 0] = xT[:, lo - 514]
    if hh == 1:
        halo = np.ascontiguousarray(xT[:, lo - 513: lo])
    qs = np.asarray(inp["L1_ffn_q_scale"], np.float32)[layer]
    ks = np.asarray(inp["L1_ffn_k_scale"], np.float32)[layer]
    return dict(
        ain=np.ascontiguousarray(ain),
        haloin=np.ascontiguousarray(halo),
        wqkv_in=np.ascontiguousarray(np.asarray(inp["L1_ffn_wqkv"], np.float32)[layer]),
        wo_in=np.ascontiguousarray(np.asarray(inp["L1_ffn_wo"], np.float32)[layer]),
        rtabs=np.ascontiguousarray(l1f_rotary_tables(hh)),
        qsks=np.ascontiguousarray(np.concatenate([np.tile(qs, 8), np.tile(ks, 8)]).reshape(1, 1024)),
        masks=np.ascontiguousarray(l1f_masks(hh)),
        cidb=np.eye(128).astype(BF16NP),
        cidf=np.eye(128, dtype=np.float32),
        ones_row_f=np.ones((1, 128), np.float32),
        consts=np.tile(np.array([[1e-5, 1.0, 1e-24, 0, 0, 0, 0, 0]], np.float32), (128, 1)),
    )


def build_l1f_attn_launch(dbg=False):
    nc = bass.Bass("TRN2", target_bir_lowering=False, debug=False)
    io = {}
    for name, shape, dt in L1FA_INPUT_SPECS:
        io[name] = nc.dram_tensor(name, shape, dt, kind="ExternalInput").ap()
    io["aout"] = nc.dram_tensor("aout", [D, TH], F32, kind="ExternalOutput").ap()
    io["dbgfa"] = nc.dram_tensor("dbgfa", [128, 256], F32, kind="ExternalOutput").ap()
    if dbg:
        io["dbgP"] = nc.dram_tensor("dbgP", [128, 1026], F32, kind="ExternalOutput").ap()
        io["dbgPt"] = nc.dram_tensor("dbgPt", [128, 1536], BF16, kind="ExternalOutput").ap()
        io["dbgKV"] = nc.dram_tensor("dbgKV", [128, 10240], BF16, kind="ExternalOutput").ap()
    with contextlib.ExitStack() as stack:
        block = stack.enter_context(nc.Block())
        ctx = Ctx(nc, block, stack)
        emit_l1f_attn(ctx, io, "fa")
    return nc


def build_lnattn_launch():
    nc = bass.Bass("TRN2", target_bir_lowering=False, debug=False)
    io = {}
    for name, shape, dt in L1FA_INPUT_SPECS:
        if name in ("ain", "haloin"):
            continue
        io[name] = nc.dram_tensor(name, shape, dt, kind="ExternalInput").ap()
    io["xin_ext"] = nc.dram_tensor("xin_ext", [XEROWS, D], F32, kind="ExternalInput").ap()
    io["amid"] = nc.dram_tensor("amid", [D, AMW], F32, kind="Internal").ap()
    io["aout"] = nc.dram_tensor("aout", [D, TH], F32, kind="ExternalOutput").ap()
    io["dbgfa"] = nc.dram_tensor("dbgfa", [128, 256], F32, kind="ExternalOutput").ap()
    with contextlib.ExitStack() as stack:
        block = stack.enter_context(nc.Block())
        ctx = Ctx(nc, block, stack)
        with contextlib.ExitStack() as lnstack:
            ctx.stack = lnstack
            emit_ln_fm(ctx, io, "ln")
        ctx.stack = stack
        emit_l1f_attn(ctx, io, "fa", src_amid=True)
    return nc


def host_lnattn_inputs(core, x3_full, inp):
    b, hh = core // 2, core % 2
    lo = hh * TH
    xe = np.zeros((XEROWS, D), np.float32)
    if hh == 1:
        xe[0:513] = x3_full[b, lo - 513:lo, :]
    xe[513:513 + TH] = x3_full[b, lo:lo + TH, :]
    m = host_l1f_attn_inputs(core, x3_full, 0, inp, None, None)
    del m["ain"]
    del m["haloin"]
    m["xin_ext"] = np.ascontiguousarray(xe)
    return m


L1FM_INPUT_SPECS = [
    ("min_", [D, TH + 1], F32),
    ("w1_in", [1024, 5632], F32),
    ("w2_in", [2816, 1024], F32),
    ("b1_col", [128, 44], F32),
    ("b2_col", [128, 8], F32),
    ("gamma_col", [128, 8], F32),
    ("beta_col", [128, 8], F32),
    ("ones_bf", [128, 1], BF16), ("ones_row_bf", [1, 128], BF16), ("consts", [128, 8], F32),
]

HID = 2730


def host_l1f_mlp_inputs(core, x_stream, layer, inp):
    b, hh = core // 2, core % 2
    lo = hh * TH
    xT = x_stream[b].T
    min_ = np.zeros((D, TH + 1), np.float32)
    min_[:, 1:] = xT[:, lo:lo + TH]
    if hh == 1:
        min_[:, 0] = xT[:, lo - 1]
    w1 = np.asarray(inp["L1_ffn_w1"], np.float32)[layer]
    w2 = np.asarray(inp["L1_ffn_w2"], np.float32)[layer]
    b1 = np.asarray(inp["L1_ffn_b1"], np.float32)[layer]
    b2 = np.asarray(inp["L1_ffn_b2"], np.float32)[layer]
    w1p = np.zeros((1024, 5632), np.float32)
    b1p = np.zeros(5632, np.float32)
    for m in range(44):
        na = min(64, max(0, HID - m * 64))
        if na > 0:
            w1p[:, m * 128: m * 128 + na] = w1[:, m * 64: m * 64 + na]
            b1p[m * 128: m * 128 + na] = b1[m * 64: m * 64 + na]
            w1p[:, m * 128 + 64: m * 128 + 64 + na] = w1[:, HID + m * 64: HID + m * 64 + na]
            b1p[m * 128 + 64: m * 128 + 64 + na] = b1[HID + m * 64: HID + m * 64 + na]
    w2p = np.zeros((2816, 1024), np.float32)
    w2p[:HID] = w2
    b1c = np.zeros((128, 44), np.float32)
    for m in range(44):
        b1c[:, m] = b1p[m * 128:(m + 1) * 128]
    return dict(
        min_=np.ascontiguousarray(min_),
        w1_in=np.ascontiguousarray(w1p),
        w2_in=np.ascontiguousarray(w2p),
        b1_col=np.ascontiguousarray(b1c),
        b2_col=np.ascontiguousarray(b2.reshape(8, 128).T),
        gamma_col=np.ascontiguousarray(np.asarray(inp["L1_ffn_ln_g"], np.float32)[layer].reshape(8, 128).T),
        beta_col=np.ascontiguousarray(np.asarray(inp["L1_ffn_ln_b"], np.float32)[layer].reshape(8, 128).T),
        ones_bf=np.ones((128, 1), BF16NP),
        ones_row_bf=np.ones((1, 128), BF16NP),
        consts=np.tile(np.array([[1e-5, 1.0, 1e-24, 0, 0, 0, 0, 0]], np.float32), (128, 1)),
    )


def build_l1f_mlp_launch(res_add=False):
    nc = bass.Bass("TRN2", target_bir_lowering=False, debug=False)
    io = {}
    for name, shape, dt in L1FM_INPUT_SPECS:
        io[name] = nc.dram_tensor(name, shape, dt, kind="ExternalInput").ap()
    if res_add:
        io["resid"] = nc.dram_tensor("resid", [D, TH], F32, kind="ExternalInput").ap()
    io["w2s"] = nc.dram_tensor("w2s", [2816, 1024], BF16, kind="Internal").ap()
    io["mout"] = nc.dram_tensor("mout", [D, TH], F32, kind="ExternalOutput").ap()
    with contextlib.ExitStack() as stack:
        block = stack.enter_context(nc.Block())
        ctx = Ctx(nc, block, stack)
        emit_l1f_mlp(ctx, io, "fm", res_add=res_add)
    return nc



def kernel(**inputs):
    inp = {k: np.asarray(v) for k, v in inputs.items()}
    x = np.ascontiguousarray(np.asarray(inp["L0_attn_x"], np.float32))
    cores = list(range(8))

    def launch(nc, maps):
        return run_bass_kernel_spmd(nc, maps, core_ids=cores).results

    res = launch(build_l0_attn_launch(), [host_l0_attn_inputs(c, x, inp) for c in cores])
    x1 = np.stack([np.concatenate([res[2 * b]["xout1"], res[2 * b + 1]["xout1"]], axis=0).T
                   for b in range(B)])

    res = launch(build_l0_ffn_launch(), [host_l0_ffn_inputs(c, x1, inp) for c in cores])
    x2 = np.stack([np.concatenate([res[2 * b]["xout"].T, res[2 * b + 1]["xout"].T], axis=0)
                   for b in range(B)])

    res = launch(build_l1_attn_launch(), [host_l1_attn_inputs(c, x2, inp) for c in cores])
    x3 = np.stack([np.concatenate([res[2 * b]["xout3"], res[2 * b + 1]["xout3"]], axis=0).T
                   for b in range(B)])

    res = launch(build_lnattn_launch(), [host_lnattn_inputs(c, x3, inp) for c in cores])
    a = np.stack([np.concatenate([res[2 * b]["aout"].T, res[2 * b + 1]["aout"].T], axis=0)
                  for b in range(B)])

    for L in range(2):
        if L > 0:
            res = launch(build_l1f_attn_launch(),
                         [host_l1f_attn_inputs(c, a, L, inp, None, None) for c in cores])
            a = np.stack([np.concatenate([res[2 * b]["aout"].T, res[2 * b + 1]["aout"].T], axis=0)
                          for b in range(B)])
        last = (L == 1)
        maps = []
        for c in cores:
            m = host_l1f_mlp_inputs(c, a, L, inp)
            if last:
                b_, hh = c // 2, c % 2
                m["resid"] = np.ascontiguousarray(x3[b_].T[:, hh * TH:(hh + 1) * TH])
            maps.append(m)
        res = launch(build_l1f_mlp_launch(res_add=last), maps)
        a = np.stack([np.concatenate([res[2 * b]["mout"].T, res[2 * b + 1]["mout"].T], axis=0)
                      for b in range(B)])
    return np.ascontiguousarray(a.astype(np.float32))
